# version 35
# baseline (speedup 1.0000x reference)
import numpy as np
import ml_dtypes
import concourse.bass as bass
import concourse.mybir as mybir
from concourse.bass_utils import run_bass_kernel_spmd
from contextlib import ExitStack

F32 = mybir.dt.float32
BF16 = mybir.dt.bfloat16
AF = mybir.ActivationFunctionType
ALU = mybir.AluOpType
AX = mybir.AxisListType

B, S, D = 4, 4096, 1024
QT = 2048
NT = 32
NTQ = 16
NCH = 8
LAMBDA_INIT = 0.8 - 0.6 * float(np.exp(-3.6))
EPS = 1e-5
SCALE = 1.0 / np.sqrt(32.0)

ENGINES = ("sync", "tensor", "vector", "scalar", "gpsimd")


class Sem:
    def __init__(self, handle, name):
        self.handle = handle
        self.name = name
        self.count = 0


class Prog:
    def __init__(self):
        self.items = {e: [] for e in ENGINES}
        self.last_wait = {}

    def op(self, eng, fn, waits=(), inc=None, incn=1):
        ws = []
        for (sv) in waits:
            s, v = sv
            if v is None or v <= 0:
                continue
            k = (eng, s.name)
            if self.last_wait.get(k, 0) >= v:
                continue
            self.last_wait[k] = v
            ws.append((s, v))
        self.items[eng].append((ws, fn, inc, incn))
        if inc is not None:
            inc.count += incn
        return None if inc is None else inc.count

    def replay(self, eng_obj, name):
        for ws, fn, inc, incn in self.items[name]:
            for (s, v) in ws:
                eng_obj.wait_ge(s.handle, v)
            instr = fn(eng_obj)
            if inc is not None:
                assert instr is not None, f"op on {name} must return an instruction"
                instr.then_inc(inc.handle, incn)


def build_program(upto="full"):
    nc = bass.Bass("TRN2", target_bir_lowering=False)

    x_d = nc.dram_tensor("x", [S, D], F32, kind="ExternalInput").ap()
    xq_d = nc.dram_tensor("xq", [QT, D], F32, kind="ExternalInput").ap()
    wq_d = nc.dram_tensor("wq", [D, D], F32, kind="ExternalInput").ap()
    wk_d = nc.dram_tensor("wk", [D, D], F32, kind="ExternalInput").ap()
    wv_d = nc.dram_tensor("wv", [D, D], F32, kind="ExternalInput").ap()
    wo_d = nc.dram_tensor("wo", [D, D], F32, kind="ExternalInput").ap()
    w0_d = nc.dram_tensor("w0", [D, D], F32, kind="ExternalInput").ap()
    w1_d = nc.dram_tensor("w1", [D, D], F32, kind="ExternalInput").ap()
    gw_d = nc.dram_tensor("gw", [D, 8], F32, kind="ExternalInput").ap()
    b0_d = nc.dram_tensor("b0", [1, D], F32, kind="ExternalInput").ap()
    b1_d = nc.dram_tensor("b1", [1, D], F32, kind="ExternalInput").ap()
    gvec_d = nc.dram_tensor("gvec", [128, 1], F32, kind="ExternalInput").ap()
    lamv_d = nc.dram_tensor("lamv", [1, 128], F32, kind="ExternalInput").ap()
    cosk_d = nc.dram_tensor("cosk", [S, 512], BF16, kind="ExternalInput").ap()
    sink_d = nc.dram_tensor("sink", [S, 512], BF16, kind="ExternalInput").ap()
    cqq_d = nc.dram_tensor("cosqq", [QT, 512], BF16, kind="ExternalInput").ap()
    sqq_d = nc.dram_tensor("sinqq", [QT, 512], BF16, kind="ExternalInput").ap()
    identb_d = nc.dram_tensor("identb", [128, 128], BF16, kind="ExternalInput").ap()
    identf_d = nc.dram_tensor("identf", [128, 128], F32, kind="ExternalInput").ap()
    ones64b_d = nc.dram_tensor("ones64b", [128, 64], BF16, kind="ExternalInput").ap()
    ones64f_d = nc.dram_tensor("ones64f", [64, 64], F32, kind="ExternalInput").ap()
    onesrow_d = nc.dram_tensor("onesrow", [1, 128], F32, kind="ExternalInput").ap()
    onescol_d = nc.dram_tensor("onescol", [128, 1], BF16, kind="ExternalInput").ap()
    istack_d = nc.dram_tensor("istack", [128, 64], F32, kind="ExternalInput").ap()

    out_d = nc.dram_tensor("out", [QT, D], F32, kind="ExternalOutput").ap()

    hfm_s = nc.dram_tensor("hfm_s", [D, S], BF16, kind="Internal").ap()
    attn_s = nc.dram_tensor("attn_s", [D, QT], BF16, kind="Internal").ap()

    prog = Prog()
    stack = ExitStack()

    sems = {}
    for nme in ("ld", "sp", "pe", "dve", "act", "pool", "smm", "se", "satt"):
        sems[nme] = Sem(stack.enter_context(nc.semaphore("s_" + nme)), nme)
    s_ld, s_sp, s_pe, s_dve, s_act, s_pool, s_smm, s_se, s_satt = (
        sems[n] for n in ("ld", "sp", "pe", "dve", "act", "pool", "smm", "se", "satt"))

    def DMA(eng, out, in_, sem, waits=()):
        allw = [(sem, sem.count)] + list(waits)
        return prog.op(eng, lambda e, o=out, i=in_: e.dma_start(out=o, in_=i),
                       waits=allw, inc=sem, incn=16)

    def T(fn, waits=(), inc=s_pe):
        return prog.op("tensor", fn, waits, inc)

    def V(fn, waits=(), inc=s_dve):
        return prog.op("vector", fn, waits, inc)

    def A(fn, waits=(), inc=s_act):
        return prog.op("scalar", fn, waits, inc)

    def P(fn, waits=(), inc=s_pool):
        return prog.op("gpsimd", fn, waits, inc)

    def _finalize_stub():
        fs = ExitStack()
        fb = fs.enter_context(nc.sbuf_tensor("finbuf", [128, D], F32))
        prev = 0
        for t in range(NTQ):
            l = DMA("sync", fb[:, :], xq_d[t * 128:(t + 1) * 128, :], s_ld,
                    waits=[(s_sp, prev)])
            prev = DMA("gpsimd", out_d[t * 128:(t + 1) * 128, :], fb[:, :], s_sp,
                       waits=[(s_ld, l)])
        prog.op("gpsimd", lambda gps: gps.nop(), waits=[(s_sp, prev)])
        fs.close()

    def _replay_and_return():
        with nc.Block() as block:
            @block.sync
            def _(sync):
                prog.replay(sync, "sync")

            @block.tensor
            def _(tensor):
                prog.replay(tensor, "tensor")

            @block.vector
            def _(vector):
                prog.replay(vector, "vector")

            @block.scalar
            def _(scalar):
                prog.replay(scalar, "scalar")

            @block.gpsimd
            def _(gpsimd):
                prog.replay(gpsimd, "gpsimd")
        stack.close()
        return nc

    identb = stack.enter_context(nc.sbuf_tensor("identb_s", [128, 128], BF16))
    ones64b = stack.enter_context(nc.sbuf_tensor("ones64b_s", [128, 64], BF16))
    ones64f = stack.enter_context(nc.sbuf_tensor("ones64f_s", [64, 64], F32))
    onesrow = stack.enter_context(nc.sbuf_tensor("onesrow_s", [1, 128], F32))
    onescol = stack.enter_context(nc.sbuf_tensor("onescol_s", [128, 1], BF16))
    istack = stack.enter_context(nc.sbuf_tensor("istack_s", [128, 64], F32))
    gvec = stack.enter_context(nc.sbuf_tensor("gvec_s", [128, 1], F32))
    gg = stack.enter_context(nc.sbuf_tensor("gg_s", [128, 1], F32))
    lamv = stack.enter_context(nc.sbuf_tensor("lamv_s", [1, 128], F32))
    lamneg_bc = stack.enter_context(nc.sbuf_tensor("lamneg_bc", [128, 1], F32))
    mneg_bc = stack.enter_context(nc.sbuf_tensor("mneg_bc", [128, 1], F32))
    svec = stack.enter_context(nc.sbuf_tensor("svec", [128, 1], F32))
    Lmat = stack.enter_context(nc.sbuf_tensor("Lmat", [128, 64], F32))

    for dst, src in ((identb, identb_d), (ones64b, ones64b_d),
                     (ones64f, ones64f_d), (onesrow, onesrow_d), (onescol, onescol_d),
                     (istack, istack_d), (gvec, gvec_d), (lamv, lamv_d)):
        DMA("sync", dst[:, :], src[:, :], s_ld)
    c_consts = s_ld.count

    c_gg = V(lambda v: v.tensor_scalar(gg[:, :], gvec[:, :], float(1.0 - LAMBDA_INIT), None,
                                       op0=ALU.mult), waits=[(s_ld, c_consts)])

    kvq_scope = ExitStack()
    Kfm = kvq_scope.enter_context(nc.sbuf_tensor("Kfm", [128, NCH * S], BF16))
    Qfm = kvq_scope.enter_context(nc.sbuf_tensor("Qfm", [128, NCH * QT], BF16))
    Vtm = kvq_scope.enter_context(nc.sbuf_tensor("Vtm", [128, NT * D], BF16))

    p2buf = ExitStack()
    xbuf = [p2buf.enter_context(nc.sbuf_tensor(f"xbuf{i}", [128, D], F32)) for i in range(2)]
    hfmb = [p2buf.enter_context(nc.sbuf_tensor(f"hfmb{i}", [128, D], BF16))
            for i in range(2)]
    ropeb = [p2buf.enter_context(nc.sbuf_tensor(f"ropeb{i}", [128, D], BF16)) for i in range(2)]
    ropeo = [p2buf.enter_context(nc.sbuf_tensor(f"ropeo{i}", [128, D], BF16)) for i in range(2)]
    ctab = [p2buf.enter_context(nc.sbuf_tensor("ctab0", [128, 512], BF16))]
    stab = [p2buf.enter_context(nc.sbuf_tensor("stab0", [128, 512], BF16))]
    qef = p2buf.enter_context(nc.sbuf_tensor("qef", [128, 512], BF16))
    qof = p2buf.enter_context(nc.sbuf_tensor("qof", [128, 512], BF16))
    pa_ = p2buf.enter_context(nc.sbuf_tensor("pa_", [128, 512], BF16))
    pb_ = p2buf.enter_context(nc.sbuf_tensor("pb_", [128, 512], BF16))
    wbuf = p2buf.enter_context(nc.sbuf_tensor("wbuf", [128, NCH * D], BF16))
    wstage = [p2buf.enter_context(nc.sbuf_tensor("wstage0", [128, 512], F32))]
    stat = p2buf.enter_context(nc.sbuf_tensor("stat", [128, 8], F32))

    p2psum = ExitStack()
    p2trps = ExitStack()
    pp = [p2psum.enter_context(nc.psum_tensor(f"pp{i}", [128, 1024], F32)) for i in range(2)]
    trb = [p2trps.enter_context(nc.psum_tensor(f"trb{i}", [128, 1024], BF16)) for i in range(2)]
    trb2 = [p2trps.enter_context(nc.psum_tensor(f"trb2{i}", [128, 1024], BF16)) for i in range(2)]

    wstage_last = [(s_dve, 0), (s_dve, 0)]

    def load_weight_bf16(w_dram, dest, dest_free=()):
        bufs = [wstage[0][:, :], xbuf[0][:, 0:512]]
        bl = [wstage_last[0], (s_dve, last_apply[0])]
        last = 0
        for ch in range(2 * NCH):
            c, hh = ch // 2, ch % 2
            b = ch % 2
            ldw = DMA("sync", bufs[b],
                      w_dram[c * 128:(c + 1) * 128, hh * 512:(hh + 1) * 512], s_ld,
                      waits=[bl[b]])
            dcol = c * D + hh * 512
            w = [(s_ld, ldw)] + list(dest_free if ch < 2 else ())
            last = V(lambda v, dcol=dcol, b=b: v.tensor_scalar(
                dest[:, dcol:dcol + 512], bufs[b], 1.0, None, op0=ALU.mult), waits=w)
            bl[b] = (s_dve, last)
        wstage_last[0] = bl[0]
        last_apply[0] = max(last_apply[0], last)
        return last

    def layernorm_stat(xb, ld_wait):
        c1 = V(lambda v: v.reduce_sum(stat[:, 0:1], xb[:, :], axis=AX.X), waits=ld_wait)
        t1 = V(lambda v: v.scalar_tensor_tensor(pa_[:, :], xb[:, 0:512], 1.0, xb[:, 0:512],
                                                op0=ALU.mult, op1=ALU.mult,
                                                accum_out=stat[:, 1:2]),
               waits=[(s_dve, last_tab_dve[0])])
        t2 = V(lambda v: v.scalar_tensor_tensor(pa_[:, :], xb[:, 512:1024], 1.0,
                                                xb[:, 512:1024], op0=ALU.mult, op1=ALU.mult,
                                                accum_out=stat[:, 7:8]), waits=[(s_dve, t1)])
        c4 = V(lambda v: v.tensor_tensor(stat[:, 1:2], stat[:, 1:2], stat[:, 7:8], op=ALU.add),
               waits=[(s_dve, t2)])
        c5 = V(lambda v: v.tensor_scalar(stat[:, 2:3], stat[:, 0:1], 1.0 / D, None, op0=ALU.mult),
               waits=[(s_dve, c1)])
        c6 = V(lambda v: v.tensor_tensor(stat[:, 3:4], stat[:, 2:3], stat[:, 2:3], op=ALU.mult),
               waits=[(s_dve, c5)])
        c7 = V(lambda v: v.tensor_scalar(stat[:, 4:5], stat[:, 1:2], 1.0 / D, float(EPS),
                                         op0=ALU.mult, op1=ALU.add),
               waits=[(s_dve, c4), (s_act, ln_act_prev[0])])
        c8 = V(lambda v: v.tensor_tensor(stat[:, 4:5], stat[:, 4:5], stat[:, 3:4],
                                         op=ALU.subtract), waits=[(s_dve, c7), (s_dve, c6)])
        return c8

    def layernorm_fin(xb, out_ap, c8cnt, apply_extra=()):
        a1 = A(lambda sc: sc.activation(stat[:, 5:6], stat[:, 4:5], AF.Ln),
               waits=[(s_dve, c8cnt), (s_dve, ln_prev_c9[0])])
        a2 = A(lambda sc: sc.activation(stat[:, 6:7], stat[:, 5:6], AF.Exp, bias=0.0, scale=-0.5),
               waits=[(s_act, a1)])
        ln_act_prev[0] = a1
        c9 = V(lambda v: v.tensor_scalar(out_ap, xb[:, :], stat[:, 2:3], stat[:, 6:7],
                                         op0=ALU.subtract, op1=ALU.mult),
               waits=[(s_act, a2)] + list(apply_extra))
        ln_prev_c9[0] = c9
        return c9

    ln_act_prev = [0]
    ln_prev_c9 = [0]

    def transpose_tile(src_bf, trbuf, dst, dst_col, first_waits, evac0_waits=()):
        CHW = dst.shape[1] // NCH
        pe_c = []
        for c in range(NCH):
            pe_c.append(T(lambda t, c=c: t.transpose(trbuf[:, c * 128:(c + 1) * 128],
                                                     src_bf[:, c * 128:(c + 1) * 128],
                                                     identb[:, :]),
                          waits=first_waits if c == 0 else ()))
        evac_c = []
        for c in range(NCH):
            evac_c.append(A(lambda sc, c=c: sc.copy(
                dst[:, c * CHW + dst_col: c * CHW + dst_col + 128],
                trbuf[:, c * 128:(c + 1) * 128]),
                waits=([(s_pe, pe_c[-1])] + list(evac0_waits)) if c == 0 else []))
        return pe_c[-1], evac_c

    def project_tile(hfm_tile, wmat_b, ppb, evac_c, extra0):
        last = None
        for c in range(NCH):
            for n in range(2):
                w = []
                if n == 0 and evac_c is not None:
                    w.append((s_act, evac_c[c]))
                if n == 0 and c == 0:
                    w += list(extra0)
                last = T(lambda t, n=n, c=c: t.matmul(
                    pp[ppb][:, n * 512:(n + 1) * 512],
                    hfm_tile[:, c * 128:(c + 1) * 128],
                    wmat_b[:, c * D + n * 512: c * D + n * 512 + 512],
                    start=(c == 0), stop=(c == NCH - 1)),
                    waits=w)
        return last

    def rope_tile(ppb, tb, mm_cnt, tab_cnt):
        e1 = A(lambda sc: sc.copy(qef[:, :], pp[ppb][:, 0:1024:2]),
               waits=[(s_pe, mm_cnt), (s_dve, last_tab_dve[0])])
        e2 = A(lambda sc: sc.copy(qof[:, :], pp[ppb][:, 1:1024:2]))
        m1 = V(lambda v: v.tensor_tensor(pa_[:, :], qef[:, :], ctab[tb][:, :], op=ALU.mult),
               waits=[(s_act, e1), (s_ld, tab_cnt)])
        m2 = V(lambda v: v.tensor_tensor(pb_[:, :], qof[:, :], stab[tb][:, :], op=ALU.mult),
               waits=[(s_act, e2)])
        o1 = V(lambda v: v.tensor_tensor(ropeo[ppb][:, 0:1024:2], pa_[:, :], pb_[:, :],
                                         op=ALU.subtract),
               waits=[(s_dve, m2), (s_pe, last_ropeo_read[ppb])])
        m3 = V(lambda v: v.tensor_tensor(pa_[:, :], qef[:, :], stab[tb][:, :], op=ALU.mult),
               waits=[(s_dve, o1)])
        m4 = V(lambda v: v.tensor_tensor(pb_[:, :], qof[:, :], ctab[tb][:, :], op=ALU.mult))
        o2 = V(lambda v: v.tensor_tensor(ropeo[ppb][:, 1:1024:2], pa_[:, :], pb_[:, :],
                                         op=ALU.add), waits=[(s_dve, m4)])
        return o2, m2, e2

    last_apply = [0, 0]
    last_evac1 = [0, 0]
    last_evac2 = [0, 0]
    last_hfmb_use = [(s_pe, 0), (s_pe, 0)]
    last_hfmb_sp = [0, 0]
    last_pp_read = [0, 0]
    last_pp_act = [0, 0]
    last_tab_pool = [0, 0]
    last_tab_dve = [0, 0]
    last_lnb_read = [0, 0]
    last_ropeo_read = [0, 0]
    last_rope_pool = 0

    load_weight_bf16(wq_d, wbuf)
    c_wq = s_dve.count

    def p2_pass(ntiles, x_src, wcnt, cD, sD, dstFM, do_spill, spill_dict):
        nonlocal last_rope_pool
        c8s, c9s, ro_s = {}, {}, {}

        def ln_stat(t):
            xb = t % 2
            ld = DMA("sync", xbuf[xb][:, :], x_src[t * 128:(t + 1) * 128, :], s_ld,
                     waits=[(s_dve, last_apply[xb])])
            c8s[t] = layernorm_stat(xbuf[xb], [(s_ld, ld)])

        def ln_fin(t):
            xb = t % 2
            c9 = layernorm_fin(xbuf[xb], ropeb[xb][:, :], c8s[t],
                               apply_extra=[(s_pe, last_lnb_read[xb])])
            last_apply[xb] = c9
            c9s[t] = c9

        def mid(t):
            nonlocal last_rope_pool
            xb = t % 2
            pe_t, ev1 = transpose_tile(ropeb[xb], trb[xb], hfmb[xb], 0,
                                       [(s_dve, c9s[t]), (s_act, last_evac1[xb])],
                                       evac0_waits=[last_hfmb_use[xb],
                                                    (s_sp, last_hfmb_sp[xb])])
            last_lnb_read[xb] = pe_t
            last_evac1[xb] = ev1[-1]
            if do_spill:
                sp = DMA("gpsimd",
                         hfm_s.rearrange("(c p) T -> p c T", p=128)[:, :, t * 128:(t + 1) * 128],
                         hfmb[xb].rearrange("p (c f) -> p c f", c=NCH)[:, :, :], s_sp,
                         waits=[(s_act, ev1[-1])])
                spill_dict[t] = sp
                last_hfmb_sp[xb] = sp
            if t >= 2:
                back(t - 2)
            if t + 2 < ntiles:
                ln_fin(t + 2)
            if t + 3 < ntiles:
                ln_stat(t + 3)
            mm = project_tile(hfmb[xb], wbuf, xb, ev1,
                              [(s_dve, max(wcnt, last_pp_read[xb])),
                               (s_act, last_pp_act[xb])])
            last_hfmb_use[xb] = (s_pe, mm)
            DMA("sync", ctab[0][:, :], cD[t * 128:(t + 1) * 128, :], s_ld,
                waits=[(s_dve, max(last_tab_pool[0], last_tab_dve[0]))])
            tl2 = DMA("sync", stab[0][:, :], sD[t * 128:(t + 1) * 128, :], s_ld,
                      waits=[(s_dve, max(last_tab_pool[0], last_tab_dve[0]))])
            ro, rpool, e2c = rope_tile(xb, 0, mm, tl2)
            last_pp_act[xb] = e2c
            last_tab_pool[0] = rpool
            last_tab_dve[0] = ro
            last_rope_pool = rpool
            ro_s[t] = ro

        def back(t):
            xb = t % 2
            pe_q, ev2 = transpose_tile(ropeo[xb], trb2[xb], dstFM, t * 128,
                                       [(s_dve, ro_s[t]), (s_act, last_evac2[xb])])
            last_ropeo_read[xb] = pe_q
            last_evac2[xb] = ev2[-1]

        ln_stat(0)
        ln_fin(0)
        if ntiles > 1:
            ln_stat(1)
            ln_fin(1)
        if ntiles > 2:
            ln_stat(2)
        for t in range(ntiles):
            mid(t)
        if ntiles > 1:
            back(ntiles - 2)
        back(ntiles - 1)

    p2_pass(NTQ, xq_d, c_wq, cqq_d, sqq_d, Qfm, False, {})

    if upto == "p2q":
        p2trps.close(); p2psum.close(); p2buf.close(); kvq_scope.close()
        _finalize_stub()
        return _replay_and_return()

    c_lastqmm = s_pe.count
    load_weight_bf16(wk_d, wbuf, dest_free=[(s_pe, c_lastqmm)])
    c_wk = s_dve.count
    spill_cnt = {}
    p2_pass(NT, x_d, c_wk, cosk_d, sink_d, Kfm, True, spill_cnt)

    if upto == "p2k":
        p2trps.close(); p2psum.close(); p2buf.close(); kvq_scope.close()
        _finalize_stub()
        return _replay_and_return()

    c_lastkmm = s_pe.count
    load_weight_bf16(wv_d, wbuf, dest_free=[(s_pe, c_lastkmm)])
    c_wv = s_dve.count
    c_qk_act = s_act.count

    p2trps.close()
    nrm = ExitStack()
    psN = [nrm.enter_context(nc.psum_tensor(f"psN{i}", [128, 512], F32)) for i in range(2)]
    gmax = nrm.enter_context(nc.sbuf_tensor("gmax", [1, 8], F32))
    _sqv0 = nrm.enter_context(nc.sbuf_tensor("sqv0", [128, 512], BF16))
    sqv = [_sqv0, pa_]
    c_rp_final = s_dve.count
    cg0 = V(lambda v: v.memset(gmax[:, :], 0.0))

    lastN_dve = cg0
    rm_bank = [0, 0]
    sq_free = [s_pe.count, s_pe.count]

    def norm_slices():
        k = [0]
        pend = []

        comb_slot = [0, 0]

        def emit_rm(bk, gcol, mm):
            nonlocal lastN_dve
            sl2 = 6 + bk
            rm = V(lambda v: v.reduce_max(gmax[:, sl2:sl2 + 1], psN[bk][0:1, :], axis=AX.X),
                   waits=[(s_pe, mm), (s_dve, comb_slot[bk])])
            rm_bank[bk] = rm
            lastN_dve = V(lambda v: v.tensor_tensor(
                gmax[:, gcol:gcol + 1], gmax[:, gcol:gcol + 1], gmax[:, sl2:sl2 + 1],
                op=ALU.max), waits=[(s_dve, rm)])
            comb_slot[bk] = lastN_dve

        def make(src, gcol, last):
            def thunk():
                bk = k[0] % 2
                k[0] += 1
                if len(pend) >= 2:
                    emit_rm(*pend.pop(0))
                cs = V(lambda v: v.tensor_tensor(sqv[bk][:, :], src, src, op=ALU.mult),
                       waits=[(s_pe, sq_free[bk]), (s_act, c_qk_act),
                              (s_dve, c_rp_final)])
                mm = T(lambda t: t.matmul(psN[bk][0:1, :], onescol[:, :], sqv[bk][:, :],
                                          start=True, stop=True),
                       waits=[(s_dve, cs), (s_dve, rm_bank[bk])])
                sq_free[bk] = mm
                pend.append((bk, gcol, mm))
                if last:
                    while pend:
                        emit_rm(*pend.pop(0))
            return thunk

        out = []
        srcs = []
        for fm, width512, gcol in ((Qfm, QT // 512, 0), (Kfm, S // 512, 1)):
            for c in range(NCH):
                for sl in range(width512):
                    srcs.append((fm[:, (c * width512 + sl) * 512:
                                    (c * width512 + sl + 1) * 512], gcol))
        for i, (src, gcol) in enumerate(srcs):
            out.append(make(src, gcol, i == len(srcs) - 1))
        return out

    nslices = norm_slices()
    last_vproj = [0, 0]
    for t in range(NT):
        hb = t % 2
        ld = DMA("sync", hfmb[hb].rearrange("p (c f) -> p c f", c=NCH)[:, :, :],
                 hfm_s.rearrange("(c p) T -> p c T", p=128)[:, :, t * 128:(t + 1) * 128],
                 s_ld, waits=[(s_sp, spill_cnt[t]), (s_pe, last_vproj[hb]),
                              last_hfmb_use[hb], (s_sp, last_hfmb_sp[hb])])
        ppb = t % 2
        mm = project_tile(hfmb[hb], wbuf, ppb, None,
                          [(s_ld, ld), (s_dve, max(c_wv, last_pp_read[ppb])),
                           (s_act, last_pp_act[ppb])])
        last_vproj[hb] = mm
        last_hfmb_use[hb] = (s_pe, mm)
        for _ in range(3):
            if nslices:
                nslices.pop(0)()
        ca = A(lambda sc, t=t, ppb=ppb: sc.copy(Vtm[:, t * D: t * D + 512],
                                                pp[ppb][:, 0:512]),
               waits=[(s_pe, mm)])
        cv = A(lambda sc, t=t, ppb=ppb: sc.copy(
            Vtm[:, t * D + 512:(t + 1) * D], pp[ppb][:, 512:1024]))
        last_pp_act[ppb] = cv
    while nslices:
        nslices.pop(0)()

    c_p2_act = s_act.count
    c_p2_dve = s_dve.count

    if upto == "p2v":
        p2psum.close(); p2buf.close(); kvq_scope.close(); nrm.close()
        _finalize_stub()
        return _replay_and_return()


    cM1 = V(lambda v: v.tensor_tensor(gmax[:, 2:3], gmax[:, 0:1], gmax[:, 1:2], op=ALU.mult),
            waits=[(s_dve, lastN_dve)])
    aM1 = A(lambda sc: sc.activation(gmax[:, 3:4], gmax[:, 2:3], AF.Ln), waits=[(s_dve, cM1)])
    aM2 = A(lambda sc: sc.activation(gmax[:, 4:5], gmax[:, 3:4], AF.Exp, bias=0.0, scale=0.5),
            waits=[(s_act, aM1)])
    cM2 = V(lambda v: v.tensor_scalar(gmax[:, 5:6], gmax[:, 4:5], float(-SCALE), None,
                                      op0=ALU.mult), waits=[(s_act, aM2)])
    bM = T(lambda t: t.matmul(psN[0][:, 0:1], onesrow[:, :], gmax[:, 5:6], start=True, stop=True),
           waits=[(s_dve, cM2), (s_dve, rm_bank[0])])
    cM3 = V(lambda v: v.tensor_copy(mneg_bc[:, :], psN[0][:, 0:1]), waits=[(s_pe, bM)])

    lrow = nrm.enter_context(nc.sbuf_tensor("lrow", [1, 8], F32))
    cL1 = V(lambda v: v.tensor_tensor(qef[0:1, 0:32],
                                      lamv[0:1, 0:32], lamv[0:1, 32:64], op=ALU.mult),
            waits=[(s_dve, last_rope_pool)])
    cL2 = V(lambda v: v.tensor_tensor(qef[0:1, 32:64], lamv[0:1, 64:96], lamv[0:1, 96:128],
                                      op=ALU.mult))
    cL3 = V(lambda v: v.reduce_sum(lrow[0:1, 0:1], qef[0:1, 0:32], axis=AX.X),
            waits=[(s_dve, cL1)])
    cL4 = V(lambda v: v.reduce_sum(lrow[0:1, 1:2], qef[0:1, 32:64], axis=AX.X),
            waits=[(s_dve, cL2)])
    aL1 = A(lambda sc: sc.activation(lrow[0:1, 2:3], lrow[0:1, 0:1], AF.Exp),
            waits=[(s_dve, cL3)])
    aL2 = A(lambda sc: sc.activation(lrow[0:1, 3:4], lrow[0:1, 1:2], AF.Exp),
            waits=[(s_dve, cL4)])
    cL5 = V(lambda v: v.tensor_tensor(lrow[0:1, 4:5], lrow[0:1, 2:3], lrow[0:1, 3:4],
                                      op=ALU.subtract), waits=[(s_act, aL2)])
    cL6 = V(lambda v: v.tensor_scalar(lrow[0:1, 5:6], lrow[0:1, 4:5], float(LAMBDA_INIT), -1.0,
                                      op0=ALU.add, op1=ALU.mult), waits=[(s_dve, cL5)])
    bL = T(lambda t: t.matmul(psN[1][:, 0:1], onesrow[:, :], lrow[0:1, 5:6], start=True, stop=True),
           waits=[(s_dve, cL6), (s_dve, rm_bank[1])])
    cL7 = V(lambda v: v.tensor_copy(lamneg_bc[:, :], psN[1][:, 0:1]), waits=[(s_pe, bL)])
    cS1 = V(lambda v: v.memset(svec[:, :], 1.0))
    cS2 = V(lambda v: v.tensor_copy(svec[64:128, :], lamneg_bc[64:128, :]),
            waits=[(s_dve, cL7), (s_dve, cS1)])
    cS3 = V(lambda v: v.tensor_scalar(Lmat[:, :], istack[:, :], svec[:, 0:1], None, op0=ALU.mult),
            waits=[(s_dve, cS2), (s_ld, c_consts)])
    c_setup_done = max(cS3, cM3, c_p2_dve)

    nrm.close()
    p2psum.close()
    p2buf.close()

    if upto == "norms":
        kvq_scope.close()
        _finalize_stub()
        return _replay_and_return()

    att = ExitStack()
    psS = att.enter_context(nc.psum_tensor("psS", [128, 2048], F32))
    psA = [att.enter_context(nc.psum_tensor(f"psA{i}", [128, 512], F32)) for i in range(2)]
    psZ = [att.enter_context(nc.psum_tensor(f"psZ{i}", [128, 512], F32)) for i in range(2)]
    pt = att.enter_context(nc.sbuf_tensor("pt", [128, 2048], BF16))
    zrbuf = att.enter_context(nc.sbuf_tensor("zrbuf", [128, 512], F32))
    tbuf = att.enter_context(nc.sbuf_tensor("tbuf", [128, 512], F32))
    abuf = att.enter_context(nc.sbuf_tensor("abuf", [64, 512], F32))
    sqf = att.enter_context(nc.sbuf_tensor("sqf", [64, 512], F32))
    ubuf = att.enter_context(nc.sbuf_tensor("ubuf", [64, 512], F32))
    wlbuf = att.enter_context(nc.sbuf_tensor("wlbuf", [64, 512], F32))
    rstdb = att.enter_context(nc.sbuf_tensor("rstdb", [64, 512], F32))
    anbuf = att.enter_context(nc.sbuf_tensor("anbuf", [64, 512], BF16))

    NQC = QT // 512
    NGRP = 16 * NQC
    combine_free = {-2: 0, -1: 0}
    prev_spill = [0]

    def make_combine(m):
        bset = m % 2
        d, qc = m // NQC, m % NQC
        st = {}

        def s1():
            st["c1"] = V(lambda v: v.reciprocal(zrbuf[:, :], psZ[bset][:, :]),
                         waits=[(s_satt, (m + 1) * NT)])

        def s2():
            st["c2"] = V(lambda v: v.tensor_tensor(tbuf[:, :], psA[bset][:, :], zrbuf[:, :],
                                                   op=ALU.mult), waits=[(s_dve, st["c1"])])

        def s3():
            st["c4"] = T(lambda t: t.matmul(psZ[bset][0:64, :], Lmat[:, :], tbuf[:, :],
                                            start=True, stop=True), waits=[(s_dve, st["c2"])])

        def s4():
            st["c5a"] = V(lambda v: v.tensor_copy(abuf[:, :], psZ[bset][0:64, :]),
                          waits=[(s_pe, st["c4"])])

        def s5():
            st["c5b"] = V(lambda v: v.tensor_tensor(sqf[:, :], abuf[:, :], abuf[:, :],
                                                    op=ALU.mult), waits=[(s_dve, st["c5a"])])

        def s6():
            st["c6"] = T(lambda t: t.matmul(psZ[bset][0:64, :], ones64f[:, :], sqf[:, :],
                                            start=True, stop=True), waits=[(s_dve, st["c5b"])])

        def s7():
            st["c7"] = V(lambda v: v.tensor_scalar(ubuf[:, :], psZ[bset][0:64, :], 1.0 / 64,
                                                   float(EPS), op0=ALU.mult, op1=ALU.add),
                         waits=[(s_pe, st["c6"])])
            combine_free[m] = st["c7"]

        def s8():
            st["c8"] = A(lambda sc: sc.activation(wlbuf[:, :], ubuf[:, :], AF.Ln),
                         waits=[(s_dve, st["c7"])])

        def s9():
            st["c9"] = A(lambda sc: sc.activation(rstdb[:, :], wlbuf[:, :], AF.Exp,
                                                  bias=0.0, scale=-0.5),
                         waits=[(s_act, st["c8"])])

        def s10():
            st["c10"] = V(lambda v: v.tensor_tensor(anbuf[:, :], abuf[:, :], rstdb[:, :],
                                                    op=ALU.mult),
                          waits=[(s_act, st["c9"]), (s_sp, prev_spill[0])])

        def s11():
            prev_spill[0] = DMA("gpsimd", attn_s[d * 64:(d + 1) * 64,
                                                 qc * 512:(qc + 1) * 512], anbuf[:, :],
                                s_sp, waits=[(s_dve, st["c10"])])

        return [s1, s2, s3, s4, s5, s6, s7, s8, s9, s10, s11]

    pending = []
    for n in range(NGRP):
        d, qc = n // NQC, n % NQC
        g = d // 2
        r0 = 32 * ((2 * d) % 4)
        r1 = r0 + 32
        aset = n % 2
        grp_exp_base = s_se.count
        if n >= 1:
            pending = make_combine(n - 1)
        for j in range(NT):
            p = j % 2
            waits = [(s_se, grp_exp_base + j - 1)]
            if j == 0:
                waits.append((s_dve, max(combine_free[n - 2], c_setup_done)))
                waits.append((s_act, c_p2_act))
            T(lambda t, p=p, r0=r0, g=g, j=j, qc=qc: t.matmul(
                psS[:, p * 1024: p * 1024 + 512],
                Kfm[r0:r0 + 32, g * S + j * 128: g * S + (j + 1) * 128],
                Qfm[r0:r0 + 32, g * QT + qc * 512: g * QT + (qc + 1) * 512],
                start=True, stop=True, tile_position=(r0, 0)), waits=waits)
            smm = T(lambda t, p=p, r1=r1, g=g, j=j, qc=qc: t.matmul(
                psS[:, p * 1024 + 512: p * 1024 + 1024],
                Kfm[r1:r1 + 32, g * S + j * 128: g * S + (j + 1) * 128],
                Qfm[r1:r1 + 32, g * QT + qc * 512: g * QT + (qc + 1) * 512],
                start=True, stop=True, tile_position=(r1, 0)), inc=s_smm)
            if j >= 1:
                jj = j - 1
                pm = jj % 2
                T(lambda t, jj=jj, pm=pm, d=d, aset=aset: t.matmul(
                    psA[aset][0:64, :], Vtm[:, jj * D + d * 64: jj * D + d * 64 + 64],
                    pt[:, pm * 1024: pm * 1024 + 512],
                    start=(jj == 0), stop=False, tile_position=(0, 0)),
                    waits=[(s_se, grp_exp_base + jj + 1)])
                T(lambda t, jj=jj, pm=pm, d=d, aset=aset: t.matmul(
                    psA[aset][64:128, :], Vtm[:, jj * D + d * 64: jj * D + d * 64 + 64],
                    pt[:, pm * 1024 + 512: pm * 1024 + 1024],
                    start=(jj == 0), stop=False, tile_position=(0, 64),
                    skip_group_check=True))
                T(lambda t, jj=jj, pm=pm, aset=aset: t.matmul(
                    psZ[aset][0:64, :], ones64b[:, :],
                    pt[:, pm * 1024: pm * 1024 + 512],
                    start=(jj == 0), stop=False, tile_position=(0, 0)))
                prog.op("tensor", lambda t, jj=jj, pm=pm, aset=aset: t.matmul(
                    psZ[aset][64:128, :], ones64b[:, :],
                    pt[:, pm * 1024 + 512: pm * 1024 + 1024],
                    start=(jj == 0), stop=False, tile_position=(0, 64),
                    skip_group_check=True), inc=s_satt)
            wexp = [(s_smm, smm)]
            if j >= 2:
                wexp.append((s_satt, s_satt.count - 1))
            A(lambda sc, p=p: sc.activation(pt[:, p * 1024: p * 1024 + 1024],
                                            psS[:, p * 1024: p * 1024 + 1024],
                                            AF.Exp, bias=mneg_bc[:, 0:1], scale=float(SCALE)),
              waits=wexp, inc=s_se)
            if pending and j >= 6 and (j % 2 == 0):
                pending.pop(0)()
        jj = NT - 1
        pm = jj % 2
        T(lambda t, jj=jj, pm=pm, d=d, aset=aset: t.matmul(
            psA[aset][0:64, :], Vtm[:, jj * D + d * 64: jj * D + d * 64 + 64],
            pt[:, pm * 1024: pm * 1024 + 512],
            start=False, stop=True, tile_position=(0, 0)),
            waits=[(s_se, grp_exp_base + jj + 1)])
        T(lambda t, jj=jj, pm=pm, d=d, aset=aset: t.matmul(
            psA[aset][64:128, :], Vtm[:, jj * D + d * 64: jj * D + d * 64 + 64],
            pt[:, pm * 1024 + 512: pm * 1024 + 1024],
            start=False, stop=True, tile_position=(0, 64), skip_group_check=True))
        T(lambda t, jj=jj, pm=pm, aset=aset: t.matmul(
            psZ[aset][0:64, :], ones64b[:, :], pt[:, pm * 1024: pm * 1024 + 512],
            start=False, stop=True, tile_position=(0, 0)))
        prog.op("tensor", lambda t, jj=jj, pm=pm, aset=aset: t.matmul(
            psZ[aset][64:128, :], ones64b[:, :], pt[:, pm * 1024 + 512: pm * 1024 + 1024],
            start=False, stop=True, tile_position=(0, 64), skip_group_check=True),
            inc=s_satt)
        while pending:
            pending.pop(0)()

    for thunk in make_combine(NGRP - 1):
        thunk()
    c_all_spills = s_sp.count

    att.close()
    kvq_scope.close()

    if upto == "att":
        _finalize_stub()
        return _replay_and_return()

    c_att_pe = s_pe.count
    c_att_act = s_act.count
    c_att_dve = s_dve.count
    p4 = ExitStack()
    wo_b = p4.enter_context(nc.sbuf_tensor("wo_b", [128, NCH * D], BF16))
    b0row = p4.enter_context(nc.sbuf_tensor("b0row", [1, D], F32))
    b1row = p4.enter_context(nc.sbuf_tensor("b1row", [1, D], F32))
    gwf = p4.enter_context(nc.sbuf_tensor("gwf", [128, 64], F32))
    identf = p4.enter_context(nc.sbuf_tensor("identf_s", [128, 128], F32))
    wst4 = [p4.enter_context(nc.sbuf_tensor(f"wst4_{i}", [128, 512], F32)) for i in range(2)]
    DMA("sync", b0row[:, :], b0_d[:, :], s_ld,
        waits=[(s_pe, c_att_pe), (s_act, c_att_act), (s_dve, c_att_dve)])
    DMA("sync", b1row[:, :], b1_d[:, :], s_ld)
    DMA("sync", identf[:, :], identf_d[:, :], s_ld)
    for cc in range(8):
        DMA("sync", gwf[:, cc * 8:(cc + 1) * 8], gw_d[cc * 128:(cc + 1) * 128, :], s_ld)
    c_gw = s_ld.count
    c_wob = 0
    _wo_stage_last = [(s_dve, 0), (s_dve, 0)]
    for ch in range(2 * NCH):
        cc, hh = ch // 2, ch % 2
        bb = ch % 2
        ldw = DMA("sync", wst4[bb][:, :],
                  wo_d[cc * 128:(cc + 1) * 128, hh * 512:(hh + 1) * 512], s_ld,
                  waits=[_wo_stage_last[bb]])
        dcol = cc * D + hh * 512
        c_wob = V(lambda v, dcol=dcol, bb=bb: v.tensor_scalar(
            wo_b[:, dcol:dcol + 512], wst4[bb][:, :], gg[:, 0:1], None, op0=ALU.mult),
            waits=[(s_ld, ldw), (s_dve, c_gg)])
        _wo_stage_last[bb] = (s_dve, c_wob)
    x1_all = p4.enter_context(nc.sbuf_tensor("x1_all", [128, NTQ * D], F32))
    h2_all = p4.enter_context(nc.sbuf_tensor("h2_all", [128, NTQ * D], BF16))
    h2f = [p4.enter_context(nc.sbuf_tensor(f"h2f{i}", [128, D], F32)) for i in range(2)]
    s01 = p4.enter_context(nc.sbuf_tensor("s01", [128, 2 * NTQ], F32))
    atile = [p4.enter_context(nc.sbuf_tensor(f"atile{i}", [128, D], BF16)) for i in range(2)]
    xqb = [p4.enter_context(nc.sbuf_tensor(f"xqb{i}", [128, D], F32)) for i in range(2)]
    h2tm = [p4.enter_context(nc.sbuf_tensor(f"h2tm{i}", [128, D], F32)) for i in range(2)]
    gstat = p4.enter_context(nc.sbuf_tensor("gstat", [128, 16], F32))
    gbuf = p4.enter_context(nc.sbuf_tensor("gbuf", [128, 8], F32))
    gbuf2 = p4.enter_context(nc.sbuf_tensor("gbuf2", [128, 8], F32))
    stat4 = p4.enter_context(nc.sbuf_tensor("stat4", [128, 8], F32))
    pax = p4.enter_context(nc.sbuf_tensor("pax", [128, 512], BF16))
    w0T = p4.enter_context(nc.sbuf_tensor("w0T", [128, NCH * D], BF16))
    w1T = p4.enter_context(nc.sbuf_tensor("w1T", [128, NCH * D], BF16))
    wrow = p4.enter_context(nc.sbuf_tensor("wrow", [128, D], BF16))

    p4a_ps = ExitStack()
    psXO = p4a_ps.enter_context(nc.psum_tensor("psXO", [128, 1024], F32))
    psTR = [p4a_ps.enter_context(nc.psum_tensor(f"psTR{i}", [128, 1024], F32))
            for i in range(2)]
    psG = p4a_ps.enter_context(nc.psum_tensor("psG", [128, 8], F32))
    trb4 = p4a_ps.enter_context(nc.psum_tensor("trb4", [128, 1024], BF16))

    ln4_act_prev = [0]
    ln4_prev_c9 = [0]

    def ln4_stat(x_ap, first_waits):
        c1 = V(lambda v: v.reduce_sum(stat4[:, 0:1], x_ap, axis=AX.X), waits=first_waits)
        t1 = V(lambda v: v.scalar_tensor_tensor(pax[:, :], x_ap[:, 0:512], 1.0, x_ap[:, 0:512],
                                                op0=ALU.mult, op1=ALU.mult,
                                                accum_out=stat4[:, 1:2]))
        t2 = V(lambda v: v.scalar_tensor_tensor(pax[:, :], x_ap[:, 512:1024], 1.0,
                                                x_ap[:, 512:1024], op0=ALU.mult, op1=ALU.mult,
                                                accum_out=stat4[:, 7:8]), waits=[(s_dve, t1)])
        c4 = V(lambda v: v.tensor_tensor(stat4[:, 1:2], stat4[:, 1:2], stat4[:, 7:8],
                                         op=ALU.add), waits=[(s_dve, t2)])
        c5 = V(lambda v: v.tensor_scalar(stat4[:, 2:3], stat4[:, 0:1], 1.0 / D, None,
                                         op0=ALU.mult), waits=[(s_dve, c1)])
        c6 = V(lambda v: v.tensor_tensor(stat4[:, 3:4], stat4[:, 2:3], stat4[:, 2:3],
                                         op=ALU.mult), waits=[(s_dve, c5)])
        c7 = V(lambda v: v.tensor_scalar(stat4[:, 4:5], stat4[:, 1:2], 1.0 / D, float(EPS),
                                         op0=ALU.mult, op1=ALU.add),
               waits=[(s_dve, c4), (s_act, ln4_act_prev[0])])
        c8 = V(lambda v: v.tensor_tensor(stat4[:, 4:5], stat4[:, 4:5], stat4[:, 3:4],
                                         op=ALU.subtract), waits=[(s_dve, c7), (s_dve, c6)])
        return c8

    def ln4_fin(x_ap, out_ap, c8cnt, apply_extra=()):
        a1 = A(lambda sc: sc.activation(stat4[:, 5:6], stat4[:, 4:5], AF.Ln),
               waits=[(s_dve, c8cnt), (s_dve, ln4_prev_c9[0])])
        a2 = A(lambda sc: sc.activation(stat4[:, 6:7], stat4[:, 5:6], AF.Exp, bias=0.0,
                                        scale=-0.5), waits=[(s_act, a1)])
        ln4_act_prev[0] = a1
        c9 = V(lambda v: v.tensor_scalar(out_ap, x_ap, stat4[:, 2:3], stat4[:, 6:7],
                                         op0=ALU.subtract, op1=ALU.mult),
               waits=[(s_act, a2)] + list(apply_extra))
        ln4_prev_c9[0] = c9
        return c9

    last_at = [0, 0]
    last_xq = [0, 0]
    last_h2tm = [0, 0]
    last_psTR = [0, 0]
    last_h2f = [0, 0]
    prev_g12 = 0
    prev_g8 = 0
    prev_g_act = 0
    prev_g_dve = 0
    prev_cx1 = 0
    p4_dmas, p4_mm, p4_c9, p4_pe, p4_ae, p4_ve, p4_gm = {}, {}, {}, {}, {}, {}, {}

    def p4_dma(t):
        ab = t % 2
        lda = DMA("sync", atile[ab].rearrange("p (c f) -> p c f", c=NCH)[:, :, :],
                  attn_s.rearrange("(c p) T -> p c T", p=128)[:, :, t * 128:(t + 1) * 128],
                  s_ld, waits=[(s_pe, last_at[ab]), (s_sp, c_all_spills)])
        ldx = DMA("sync", xqb[ab][:, :], xq_d[t * 128:(t + 1) * 128, :], s_ld,
                  waits=[(s_dve, last_xq[ab])])
        p4_dmas[t] = (lda, ldx)

    def p4_mmwo(t):
        ab = t % 2
        mm = None
        for n in range(2):
            for c in range(NCH):
                w = [(s_ld, p4_dmas[t][0]),
                     (s_dve, max(c_wob, prev_cx1))] if (n == 0 and c == 0) else ()
                mm = T(lambda t_, n=n, c=c, ab=ab: t_.matmul(
                    psXO[:, n * 512:(n + 1) * 512], atile[ab][:, c * 128:(c + 1) * 128],
                    wo_b[:, c * D + n * 512: c * D + n * 512 + 512],
                    start=(c == 0), stop=(c == NCH - 1)), waits=w)
        last_at[ab] = mm
        p4_mm[t] = mm

    def p4_ln(t):
        nonlocal prev_cx1
        ab = t % 2
        cx1 = V(lambda v, t=t, ab=ab: v.tensor_tensor(x1_all[:, t * D:(t + 1) * D],
                                                      psXO[:, :], xqb[ab][:, :],
                                                      op=ALU.add),
                waits=[(s_pe, p4_mm[t]), (s_ld, p4_dmas[t][1])])
        prev_cx1 = cx1
        last_xq[ab] = cx1
        c8 = ln4_stat(x1_all[:, t * D:(t + 1) * D], [(s_dve, cx1)])
        c9 = ln4_fin(x1_all[:, t * D:(t + 1) * D], h2tm[ab][:, :], c8,
                     apply_extra=[(s_pe, last_h2tm[ab])])
        p4_c9[t] = c9

    def p4_tr(t):
        ab = t % 2
        pe_last = None
        for c in range(NCH):
            w = [(s_dve, p4_c9[t]), (s_dve, last_psTR[ab])] if c == 0 else ()
            pe_last = T(lambda t_, c=c, ab=ab: t_.transpose(
                psTR[ab][:, c * 128:(c + 1) * 128],
                h2tm[ab][:, c * 128:(c + 1) * 128],
                identf[:, :]), waits=w)
        last_h2tm[ab] = pe_last
        aev = None
        for c in range(NCH):
            aev = A(lambda sc, c=c, t=t, ab=ab: sc.copy(
                h2_all[:, t * D + c * 128: t * D + (c + 1) * 128],
                psTR[ab][:, c * 128:(c + 1) * 128]),
                waits=[(s_pe, pe_last)] if c == 0 else ())
        p4_pe[t] = pe_last
        p4_ae[t] = aev

    def p4_vev(t):
        ab = t % 2
        vev = None
        for c in range(NCH):
            vev = V(lambda v, c=c, ab=ab: v.tensor_copy(
                h2f[ab][:, c * 128:(c + 1) * 128],
                psTR[ab][:, c * 128:(c + 1) * 128]),
                waits=[(s_pe, p4_pe[t]), (s_act, p4_ae[t]),
                       (s_pe, last_h2f[ab])] if c == 0 else ())
        last_psTR[ab] = vev
        p4_ve[t] = vev

    def p4_gmm(t):
        ab = t % 2
        gmm = None
        for c in range(NCH):
            w = [(s_dve, p4_ve[t]), (s_ld, c_gw), (s_dve, prev_g8),
                 (s_act, prev_g_act)] if c == 0 else ()
            gmm = T(lambda t_, c=c, ab=ab: t_.matmul(
                psG[:, :], h2f[ab][:, c * 128:(c + 1) * 128],
                gwf[:, c * 8:(c + 1) * 8],
                start=(c == 0), stop=(c == NCH - 1)), waits=w)
        last_h2f[ab] = gmm
        p4_gm[t] = gmm

    def p4_gchain(t):
        nonlocal prev_g12, prev_g8, prev_g_act, prev_g_dve
        gmm = p4_gm[t]
        g1 = V(lambda v: v.reduce_max(gstat[:, 0:1], psG[:, :], axis=AX.X),
               waits=[(s_pe, gmm), (s_act, prev_g12)])
        g2 = V(lambda v: v.tensor_scalar(gstat[:, 1:2], gstat[:, 0:1], -1.0, None,
                                         op0=ALU.mult),
               waits=[(s_dve, g1), (s_act, prev_g_act)])
        g3 = A(lambda sc: sc.activation(gbuf[:, :], psG[:, :], AF.Exp, bias=gstat[:, 1:2],
                                        scale=1.0), waits=[(s_dve, g2)])
        g4 = V(lambda v: v.reduce_sum(gstat[:, 2:3], gbuf[:, :], axis=AX.X),
               waits=[(s_act, g3)])
        g5a = A(lambda sc: sc.activation(gstat[:, 10:11], gstat[:, 2:3], AF.Ln),
                waits=[(s_dve, g4)])
        g5 = A(lambda sc: sc.activation(gstat[:, 3:4], gstat[:, 10:11], AF.Exp,
                                        bias=0.0, scale=-1.0), waits=[(s_act, g5a)])
        g6 = V(lambda v: v.tensor_scalar(gstat[:, 4:5], psG[:, 0:1], gstat[:, 0:1], None,
                                         op0=ALU.is_equal), waits=[(s_dve, g1), (s_act, g3)])
        g7 = V(lambda v: v.tensor_scalar(gbuf2[:, :], psG[:, :], gstat[:, 0:1], None,
                                         op0=ALU.is_equal), waits=[(s_act, g3)])
        g8 = V(lambda v: v.scalar_tensor_tensor(gbuf2[:, :], gbuf2[:, :], -1e30, psG[:, :],
                                                op0=ALU.mult, op1=ALU.add),
               waits=[(s_dve, g7)])
        g9 = V(lambda v: v.reduce_max(gstat[:, 5:6], gbuf2[:, :], axis=AX.X),
               waits=[(s_dve, g8)])
        g10 = V(lambda v: v.tensor_tensor(gstat[:, 6:7], gbuf2[:, 1:2], gstat[:, 5:6],
                                          op=ALU.is_equal), waits=[(s_dve, g9)])
        g11 = V(lambda v: v.tensor_tensor(gstat[:, 7:8], gstat[:, 5:6], gstat[:, 0:1],
                                          op=ALU.subtract), waits=[(s_dve, g9)])
        g12 = A(lambda sc: sc.activation(gstat[:, 8:9], gstat[:, 7:8], AF.Exp),
                waits=[(s_dve, g11)])
        prev_g12 = g12
        g13 = V(lambda v, t=t: v.tensor_tensor(s01[:, 2 * t:2 * t + 1], gstat[:, 4:5],
                                               gstat[:, 3:4], op=ALU.mult),
                waits=[(s_dve, g6), (s_act, g5)])
        g14 = V(lambda v: v.tensor_tensor(gstat[:, 9:10], gstat[:, 6:7], gstat[:, 8:9],
                                          op=ALU.mult), waits=[(s_act, g12), (s_dve, g10)])
        g15 = V(lambda v, t=t: v.tensor_tensor(s01[:, 2 * t + 1:2 * t + 2], gstat[:, 9:10],
                                               gstat[:, 3:4], op=ALU.mult),
                waits=[(s_dve, g14)])
        prev_g8 = g8
        prev_g_act = g3
        prev_g_dve = g15

    wT_state = {"cast": 0, "evl": 0, "trs": 0, "ldp": c_wob}

    def wT_row(idx):
        wd, dest = (w0_d, w0T) if idx < NCH else (w1_d, w1T)
        r = idx % NCH
        lds = 0
        for hh in range(2):
            lds = DMA("sync", wst4[hh][:, :],
                      wd[r * 128:(r + 1) * 128, hh * 512:(hh + 1) * 512], s_ld,
                      waits=[(s_dve, max(wT_state["cast"], wT_state["ldp"]))])
        cast0 = V(lambda v: v.tensor_copy(wrow[:, 0:512], wst4[0][:, :]),
                  waits=[(s_ld, lds), (s_pe, wT_state["trs"])])
        cast = V(lambda v: v.tensor_copy(wrow[:, 512:1024], wst4[1][:, :]),
                 waits=[(s_dve, cast0)])
        pl = None
        for c in range(NCH):
            w = [(s_dve, cast), (s_act, wT_state["evl"])] if c == 0 else ()
            pl = T(lambda t_, c=c: t_.transpose(trb4[:, c * 128:(c + 1) * 128],
                                                wrow[:, c * 128:(c + 1) * 128],
                                                identb[:, :]), waits=w)
        evl = None
        for c in range(NCH):
            evl = A(lambda sc, c=c, r=r, dest=dest: sc.copy(
                dest[:, c * D + r * 128: c * D + (r + 1) * 128],
                trb4[:, c * 128:(c + 1) * 128]),
                waits=[(s_pe, pl)] if c == 0 else ())
        wT_state["cast"] = cast
        wT_state["evl"] = evl
        wT_state["trs"] = pl
        return evl

    p4_dma(0)
    if NTQ > 1:
        p4_dma(1)
    p4_mmwo(0)
    c_w1T = 0
    for t in range(NTQ):
        if t >= 3:
            p4_gchain(t - 3)
        if t < 2 * NCH:
            c_w1T = wT_row(t)
        p4_ln(t)
        if t + 1 < NTQ:
            p4_mmwo(t + 1)
        if t + 2 < NTQ:
            p4_dma(t + 2)
        if t >= 1:
            p4_tr(t - 1)
        if t >= 2:
            p4_gmm(t - 2)
        if t >= 1:
            p4_vev(t - 1)
    p4_tr(NTQ - 1)
    p4_vev(NTQ - 1)
    if NTQ >= 3:
        p4_gchain(NTQ - 3)
    p4_gmm(NTQ - 2)
    if NTQ >= 2:
        p4_gchain(NTQ - 2)
    p4_gmm(NTQ - 1)
    p4_gchain(NTQ - 1)
    for idx in range(min(NTQ, 2 * NCH), 2 * NCH):
        c_w1T = wT_row(idx)

    c_4a_act = s_act.count
    c_4a_dve = s_dve.count
    p4a_ps.close()

    p4b_ps = ExitStack()
    psE = [p4b_ps.enter_context(nc.psum_tensor(f"psE{i}", [128, 1024], F32)) for i in range(2)]
    ebuf = [p4.enter_context(nc.sbuf_tensor(f"ebuf{i}", [128, D], F32)) for i in range(2)]
    ybuf = [p4.enter_context(nc.sbuf_tensor(f"ybuf{i}", [128, D], F32)) for i in range(2)]
    last_sil = [0, 0]
    last_store = [0, 0]
    last_fb = 0
    for t in range(NTQ):
        for e in range(2):
            wT = w0T if e == 0 else w1T
            brow = b0row if e == 0 else b1row
            mm = None
            for n in range(2):
                for c in range(NCH):
                    w = ()
                    if n == 0 and c == 0:
                        w = [(s_act, max(last_sil[e], c_w1T, c_4a_act)), (s_ld, c_consts)]
                    mm = T(lambda t_, n=n, c=c, t=t, wT=wT, e=e: t_.matmul(
                        psE[e][:, n * 512:(n + 1) * 512],
                        h2_all[:, t * D + c * 128: t * D + (c + 1) * 128],
                        wT[:, c * D + n * 512: c * D + n * 512 + 512],
                        start=(c == 0), stop=(c == NCH - 1)), waits=w)
                mm = T(lambda t_, n=n, e=e, brow=brow: t_.matmul(
                    psE[e][:, n * 512:(n + 1) * 512], onesrow[:, :],
                    brow[0:1, n * 512:(n + 1) * 512], start=False, stop=False,
                    skip_group_check=True))
            last_sil[e] = A(lambda sc, e=e: sc.activation(ebuf[e][:, :], psE[e][:, :], AF.Silu),
                            waits=[(s_pe, mm), (s_dve, last_fb)])
        yb = t % 2
        f1 = V(lambda v, t=t: v.tensor_scalar(ebuf[0][:, :], ebuf[0][:, :],
                                              s01[:, 2 * t:2 * t + 1], None, op0=ALU.mult),
               waits=[(s_act, last_sil[1]), (s_sp, max(last_store))])
        f2 = V(lambda v, t=t, yb=yb: v.scalar_tensor_tensor(
            ybuf[yb][:, :], ebuf[1][:, :], s01[:, 2 * t + 1:2 * t + 2], ebuf[0][:, :],
            op0=ALU.mult, op1=ALU.add), waits=[(s_dve, f1)])
        f3 = V(lambda v, t=t, yb=yb: v.tensor_tensor(ybuf[yb][:, :], ybuf[yb][:, :],
                                                     x1_all[:, t * D:(t + 1) * D], op=ALU.add),
               waits=[(s_dve, f2)])
        last_fb = f3
        st = DMA("gpsimd", out_d[t * 128:(t + 1) * 128, :], ybuf[yb][:, :], s_sp,
                 waits=[(s_dve, f3)])
        last_store[yb] = st

    final_store = s_sp.count
    prog.op("gpsimd", lambda gps: gps.nop(), waits=[(s_sp, final_store)])

    p4b_ps.close()
    p4.close()

    return _replay_and_return()


def _rope_tables():
    d = 32
    e = (2.0 * np.arange(d // 2, dtype=np.float32) / np.float32(d)).astype(np.float32)
    freq = (np.float32(1.0) / (np.float32(0.0001) ** e)).astype(np.float32)
    pos = np.arange(S, dtype=np.float32)[:, None]
    ang = (pos * freq[None, :]).astype(np.float32)
    cos = np.cos(ang).astype(np.float32)
    sin = np.sin(ang).astype(np.float32)
    return np.tile(cos, (1, 32)), np.tile(sin, (1, 32))


_NC_CACHE = {}


def kernel(L0_attn_x, L0_attn_Wq, L0_attn_Wk, L0_attn_Wv, L0_attn_Wo,
           L0_attn_lambda_q1, L0_attn_lambda_k1, L0_attn_lambda_q2, L0_attn_lambda_k2,
           L0_attn_subln_g, L0_ffn_gate_w, L0_ffn_w0, L0_ffn_b0, L0_ffn_w1, L0_ffn_b1):
    x = np.ascontiguousarray(np.asarray(L0_attn_x, dtype=np.float32))
    Wq = np.ascontiguousarray(np.asarray(L0_attn_Wq, dtype=np.float32))
    Wk = np.ascontiguousarray(np.asarray(L0_attn_Wk, dtype=np.float32))
    Wv = np.ascontiguousarray(np.asarray(L0_attn_Wv, dtype=np.float32))
    Wo = np.ascontiguousarray(np.asarray(L0_attn_Wo, dtype=np.float32))
    w0 = np.ascontiguousarray(np.asarray(L0_ffn_w0, dtype=np.float32))
    w1 = np.ascontiguousarray(np.asarray(L0_ffn_w1, dtype=np.float32))
    gw = np.ascontiguousarray(np.asarray(L0_ffn_gate_w, dtype=np.float32))
    b0 = np.asarray(L0_ffn_b0, dtype=np.float32).reshape(1, D)
    b1 = np.asarray(L0_ffn_b1, dtype=np.float32).reshape(1, D)
    g = np.asarray(L0_attn_subln_g, dtype=np.float32)
    lamv = np.concatenate([np.asarray(L0_attn_lambda_q1, dtype=np.float32),
                           np.asarray(L0_attn_lambda_k1, dtype=np.float32),
                           np.asarray(L0_attn_lambda_q2, dtype=np.float32),
                           np.asarray(L0_attn_lambda_k2, dtype=np.float32)]).reshape(1, 128)
    cosT, sinT = _rope_tables()
    coskb = cosT.astype(ml_dtypes.bfloat16)
    sinkb = sinT.astype(ml_dtypes.bfloat16)

    identb = np.eye(128, dtype=np.float32).astype(ml_dtypes.bfloat16)
    identf = np.eye(128, dtype=np.float32)
    ones64b = np.ones((128, 64), dtype=np.float32).astype(ml_dtypes.bfloat16)
    ones64f = np.ones((64, 64), dtype=np.float32)
    onesrow = np.ones((1, 128), dtype=np.float32)
    onescol = np.ones((128, 1), dtype=np.float32).astype(ml_dtypes.bfloat16)
    istack = np.concatenate([np.eye(64, dtype=np.float32),
                             np.eye(64, dtype=np.float32)], axis=0)
    gvec = np.tile(g, 2).reshape(128, 1).astype(np.float32)

    if "nc" not in _NC_CACHE:
        _NC_CACHE["nc"] = build_program()
    nc = _NC_CACHE["nc"]

    in_maps = []
    for core in range(8):
        b = core // 2
        qh = core % 2
        q0 = qh * QT
        in_maps.append({
            "x": x[b],
            "xq": np.ascontiguousarray(x[b, q0:q0 + QT, :]),
            "wq": Wq, "wk": Wk, "wv": Wv, "wo": Wo,
            "w0": w0, "w1": w1, "gw": gw, "b0": b0, "b1": b1,
            "gvec": gvec, "lamv": lamv,
            "cosk": coskb, "sink": sinkb,
            "cosqq": np.ascontiguousarray(coskb[q0:q0 + QT, :]),
            "sinqq": np.ascontiguousarray(sinkb[q0:q0 + QT, :]),
            "identb": identb, "identf": identf,
            "ones64b": ones64b, "ones64f": ones64f,
            "onesrow": onesrow, "onescol": onescol,
            "istack": istack,
        })

    res = run_bass_kernel_spmd(nc, in_maps, core_ids=list(range(8)))

    out = np.empty((B, S, D), dtype=np.float32)
    for core in range(8):
        b = core // 2
        qh = core % 2
        out[b, qh * QT:(qh + 1) * QT, :] = res.results[core]["out"]
    return out


# version 36 (final)
# speedup vs baseline: 1.0193x; 1.0193x over previous
import numpy as np
import ml_dtypes
import concourse.bass as bass
import concourse.mybir as mybir
from concourse.bass_utils import run_bass_kernel_spmd
from contextlib import ExitStack

F32 = mybir.dt.float32
BF16 = mybir.dt.bfloat16
AF = mybir.ActivationFunctionType
ALU = mybir.AluOpType
AX = mybir.AxisListType

B, S, D = 4, 4096, 1024
QT = 2048
NT = 32
NTQ = 16
NCH = 8
LAMBDA_INIT = 0.8 - 0.6 * float(np.exp(-3.6))
EPS = 1e-5
SCALE = 1.0 / np.sqrt(32.0)

ENGINES = ("sync", "tensor", "vector", "scalar", "gpsimd")


class Sem:
    def __init__(self, handle, name):
        self.handle = handle
        self.name = name
        self.count = 0


class Prog:
    def __init__(self):
        self.items = {e: [] for e in ENGINES}
        self.last_wait = {}

    def op(self, eng, fn, waits=(), inc=None, incn=1):
        ws = []
        for (sv) in waits:
            s, v = sv
            if v is None or v <= 0:
                continue
            k = (eng, s.name)
            if self.last_wait.get(k, 0) >= v:
                continue
            self.last_wait[k] = v
            ws.append((s, v))
        self.items[eng].append((ws, fn, inc, incn))
        if inc is not None:
            inc.count += incn
        return None if inc is None else inc.count

    def replay(self, eng_obj, name):
        for ws, fn, inc, incn in self.items[name]:
            for (s, v) in ws:
                eng_obj.wait_ge(s.handle, v)
            instr = fn(eng_obj)
            if inc is not None:
                assert instr is not None, f"op on {name} must return an instruction"
                instr.then_inc(inc.handle, incn)


def build_program(upto="full"):
    nc = bass.Bass("TRN2", target_bir_lowering=False)

    x_d = nc.dram_tensor("x", [S, D], F32, kind="ExternalInput").ap()
    xq_d = nc.dram_tensor("xq", [QT, D], F32, kind="ExternalInput").ap()
    wq_d = nc.dram_tensor("wq", [D, D], F32, kind="ExternalInput").ap()
    wk_d = nc.dram_tensor("wk", [D, D], F32, kind="ExternalInput").ap()
    wv_d = nc.dram_tensor("wv", [D, D], F32, kind="ExternalInput").ap()
    wo_d = nc.dram_tensor("wo", [D, D], F32, kind="ExternalInput").ap()
    w0_d = nc.dram_tensor("w0", [D, D], F32, kind="ExternalInput").ap()
    w1_d = nc.dram_tensor("w1", [D, D], F32, kind="ExternalInput").ap()
    gw_d = nc.dram_tensor("gw", [D, 8], F32, kind="ExternalInput").ap()
    b0_d = nc.dram_tensor("b0", [1, D], F32, kind="ExternalInput").ap()
    b1_d = nc.dram_tensor("b1", [1, D], F32, kind="ExternalInput").ap()
    gvec_d = nc.dram_tensor("gvec", [128, 1], F32, kind="ExternalInput").ap()
    lamv_d = nc.dram_tensor("lamv", [1, 128], F32, kind="ExternalInput").ap()
    cosk_d = nc.dram_tensor("cosk", [S, 512], BF16, kind="ExternalInput").ap()
    sink_d = nc.dram_tensor("sink", [S, 512], BF16, kind="ExternalInput").ap()
    cqq_d = nc.dram_tensor("cosqq", [QT, 512], BF16, kind="ExternalInput").ap()
    sqq_d = nc.dram_tensor("sinqq", [QT, 512], BF16, kind="ExternalInput").ap()
    identb_d = nc.dram_tensor("identb", [128, 128], BF16, kind="ExternalInput").ap()
    identf_d = nc.dram_tensor("identf", [128, 128], F32, kind="ExternalInput").ap()
    ones64b_d = nc.dram_tensor("ones64b", [128, 64], BF16, kind="ExternalInput").ap()
    ones64f_d = nc.dram_tensor("ones64f", [64, 64], F32, kind="ExternalInput").ap()
    onesrow_d = nc.dram_tensor("onesrow", [1, 128], F32, kind="ExternalInput").ap()
    onescol_d = nc.dram_tensor("onescol", [128, 1], BF16, kind="ExternalInput").ap()
    istack_d = nc.dram_tensor("istack", [128, 64], F32, kind="ExternalInput").ap()

    out_d = nc.dram_tensor("out", [QT, D], F32, kind="ExternalOutput").ap()

    hfm_s = nc.dram_tensor("hfm_s", [D, S], BF16, kind="Internal").ap()
    attn_s = nc.dram_tensor("attn_s", [D, QT], BF16, kind="Internal").ap()

    prog = Prog()
    stack = ExitStack()

    sems = {}
    for nme in ("ld", "sp", "pe", "dve", "act", "pool", "smm", "se", "satt"):
        sems[nme] = Sem(stack.enter_context(nc.semaphore("s_" + nme)), nme)
    s_ld, s_sp, s_pe, s_dve, s_act, s_pool, s_smm, s_se, s_satt = (
        sems[n] for n in ("ld", "sp", "pe", "dve", "act", "pool", "smm", "se", "satt"))

    def DMA(eng, out, in_, sem, waits=()):
        allw = [(sem, sem.count)] + list(waits)
        return prog.op(eng, lambda e, o=out, i=in_: e.dma_start(out=o, in_=i),
                       waits=allw, inc=sem, incn=16)

    def T(fn, waits=(), inc=s_pe):
        return prog.op("tensor", fn, waits, inc)

    def V(fn, waits=(), inc=s_dve):
        return prog.op("vector", fn, waits, inc)

    def A(fn, waits=(), inc=s_act):
        return prog.op("scalar", fn, waits, inc)

    def P(fn, waits=(), inc=s_pool):
        return prog.op("gpsimd", fn, waits, inc)

    def _finalize_stub():
        fs = ExitStack()
        fb = fs.enter_context(nc.sbuf_tensor("finbuf", [128, D], F32))
        prev = 0
        for t in range(NTQ):
            l = DMA("sync", fb[:, :], xq_d[t * 128:(t + 1) * 128, :], s_ld,
                    waits=[(s_sp, prev)])
            prev = DMA("gpsimd", out_d[t * 128:(t + 1) * 128, :], fb[:, :], s_sp,
                       waits=[(s_ld, l)])
        prog.op("gpsimd", lambda gps: gps.nop(), waits=[(s_sp, prev)])
        fs.close()

    def _replay_and_return():
        with nc.Block() as block:
            @block.sync
            def _(sync):
                prog.replay(sync, "sync")

            @block.tensor
            def _(tensor):
                prog.replay(tensor, "tensor")

            @block.vector
            def _(vector):
                prog.replay(vector, "vector")

            @block.scalar
            def _(scalar):
                prog.replay(scalar, "scalar")

            @block.gpsimd
            def _(gpsimd):
                prog.replay(gpsimd, "gpsimd")
        stack.close()
        return nc

    identb = stack.enter_context(nc.sbuf_tensor("identb_s", [128, 128], BF16))
    ones64b = stack.enter_context(nc.sbuf_tensor("ones64b_s", [128, 64], BF16))
    ones64f = stack.enter_context(nc.sbuf_tensor("ones64f_s", [64, 64], F32))
    onesrow = stack.enter_context(nc.sbuf_tensor("onesrow_s", [1, 128], F32))
    onescol = stack.enter_context(nc.sbuf_tensor("onescol_s", [128, 1], BF16))
    istack = stack.enter_context(nc.sbuf_tensor("istack_s", [128, 64], F32))
    gvec = stack.enter_context(nc.sbuf_tensor("gvec_s", [128, 1], F32))
    gg = stack.enter_context(nc.sbuf_tensor("gg_s", [128, 1], F32))
    lamv = stack.enter_context(nc.sbuf_tensor("lamv_s", [1, 128], F32))
    lamneg_bc = stack.enter_context(nc.sbuf_tensor("lamneg_bc", [128, 1], F32))
    mneg_bc = stack.enter_context(nc.sbuf_tensor("mneg_bc", [128, 1], F32))
    svec = stack.enter_context(nc.sbuf_tensor("svec", [128, 1], F32))
    Lmat = stack.enter_context(nc.sbuf_tensor("Lmat", [128, 64], F32))

    for dst, src in ((identb, identb_d), (ones64b, ones64b_d),
                     (ones64f, ones64f_d), (onesrow, onesrow_d), (onescol, onescol_d),
                     (istack, istack_d), (gvec, gvec_d), (lamv, lamv_d)):
        DMA("sync", dst[:, :], src[:, :], s_ld)
    c_consts = s_ld.count

    c_gg = V(lambda v: v.tensor_scalar(gg[:, :], gvec[:, :], float(1.0 - LAMBDA_INIT), None,
                                       op0=ALU.mult), waits=[(s_ld, c_consts)])

    kvq_scope = ExitStack()
    Kfm = kvq_scope.enter_context(nc.sbuf_tensor("Kfm", [128, NCH * S], BF16))
    Qfm = kvq_scope.enter_context(nc.sbuf_tensor("Qfm", [128, NCH * QT], BF16))
    Vtm = kvq_scope.enter_context(nc.sbuf_tensor("Vtm", [128, NT * D], BF16))

    p2buf = ExitStack()
    xbuf = [p2buf.enter_context(nc.sbuf_tensor(f"xbuf{i}", [128, D], F32)) for i in range(2)]
    hfmb = [p2buf.enter_context(nc.sbuf_tensor(f"hfmb{i}", [128, D], BF16))
            for i in range(2)]
    ropeb = [p2buf.enter_context(nc.sbuf_tensor(f"ropeb{i}", [128, D], BF16)) for i in range(2)]
    ropeo = [p2buf.enter_context(nc.sbuf_tensor(f"ropeo{i}", [128, D], BF16)) for i in range(2)]
    ctab = [p2buf.enter_context(nc.sbuf_tensor("ctab0", [128, 512], BF16))]
    stab = [p2buf.enter_context(nc.sbuf_tensor("stab0", [128, 512], BF16))]
    qef = p2buf.enter_context(nc.sbuf_tensor("qef", [128, 512], BF16))
    qof = p2buf.enter_context(nc.sbuf_tensor("qof", [128, 512], BF16))
    pa_ = p2buf.enter_context(nc.sbuf_tensor("pa_", [128, 512], BF16))
    pb_ = p2buf.enter_context(nc.sbuf_tensor("pb_", [128, 512], BF16))
    wbuf = p2buf.enter_context(nc.sbuf_tensor("wbuf", [128, NCH * D], BF16))
    wstage = [p2buf.enter_context(nc.sbuf_tensor("wstage0", [128, 512], F32))]
    stat = p2buf.enter_context(nc.sbuf_tensor("stat", [128, 8], F32))

    p2psum = ExitStack()
    p2trps = ExitStack()
    pp = [p2psum.enter_context(nc.psum_tensor(f"pp{i}", [128, 1024], F32)) for i in range(2)]
    trb = [p2trps.enter_context(nc.psum_tensor(f"trb{i}", [128, 1024], BF16)) for i in range(2)]
    trb2 = [p2trps.enter_context(nc.psum_tensor(f"trb2{i}", [128, 1024], BF16)) for i in range(2)]

    wstage_last = [(s_dve, 0), (s_dve, 0)]

    def load_weight_bf16(w_dram, dest, dest_free=()):
        bufs = [wstage[0][:, :], xbuf[0][:, 0:512]]
        bl = [wstage_last[0], (s_dve, last_apply[0])]
        last = 0
        for ch in range(2 * NCH):
            c, hh = ch // 2, ch % 2
            b = ch % 2
            ldw = DMA("sync", bufs[b],
                      w_dram[c * 128:(c + 1) * 128, hh * 512:(hh + 1) * 512], s_ld,
                      waits=[bl[b]])
            dcol = c * D + hh * 512
            w = [(s_ld, ldw)] + list(dest_free if ch < 2 else ())
            last = V(lambda v, dcol=dcol, b=b: v.tensor_scalar(
                dest[:, dcol:dcol + 512], bufs[b], 1.0, None, op0=ALU.mult), waits=w)
            bl[b] = (s_dve, last)
        wstage_last[0] = bl[0]
        last_apply[0] = max(last_apply[0], last)
        return last

    def layernorm_stat(xb, ld_wait):
        c1 = V(lambda v: v.reduce_sum(stat[:, 0:1], xb[:, :], axis=AX.X), waits=ld_wait)
        t1 = V(lambda v: v.scalar_tensor_tensor(pa_[:, :], xb[:, 0:512], 1.0, xb[:, 0:512],
                                                op0=ALU.mult, op1=ALU.mult,
                                                accum_out=stat[:, 1:2]),
               waits=[(s_dve, last_tab_dve[0])])
        t2 = V(lambda v: v.scalar_tensor_tensor(pa_[:, :], xb[:, 512:1024], 1.0,
                                                xb[:, 512:1024], op0=ALU.mult, op1=ALU.mult,
                                                accum_out=stat[:, 7:8]), waits=[(s_dve, t1)])
        c4 = V(lambda v: v.tensor_tensor(stat[:, 1:2], stat[:, 1:2], stat[:, 7:8], op=ALU.add),
               waits=[(s_dve, t2)])
        c5 = V(lambda v: v.tensor_scalar(stat[:, 2:3], stat[:, 0:1], 1.0 / D, None, op0=ALU.mult),
               waits=[(s_dve, c1)])
        c6 = V(lambda v: v.tensor_tensor(stat[:, 3:4], stat[:, 2:3], stat[:, 2:3], op=ALU.mult),
               waits=[(s_dve, c5)])
        c7 = V(lambda v: v.tensor_scalar(stat[:, 4:5], stat[:, 1:2], 1.0 / D, float(EPS),
                                         op0=ALU.mult, op1=ALU.add),
               waits=[(s_dve, c4), (s_act, ln_act_prev[0])])
        c8 = V(lambda v: v.tensor_tensor(stat[:, 4:5], stat[:, 4:5], stat[:, 3:4],
                                         op=ALU.subtract), waits=[(s_dve, c7), (s_dve, c6)])
        return c8

    def layernorm_fin(xb, out_ap, c8cnt, apply_extra=()):
        a1 = A(lambda sc: sc.activation(stat[:, 5:6], stat[:, 4:5], AF.Ln),
               waits=[(s_dve, c8cnt), (s_dve, ln_prev_c9[0])])
        a2 = A(lambda sc: sc.activation(stat[:, 6:7], stat[:, 5:6], AF.Exp, bias=0.0, scale=-0.5),
               waits=[(s_act, a1)])
        ln_act_prev[0] = a1
        c9 = V(lambda v: v.tensor_scalar(out_ap, xb[:, :], stat[:, 2:3], stat[:, 6:7],
                                         op0=ALU.subtract, op1=ALU.mult),
               waits=[(s_act, a2)] + list(apply_extra))
        ln_prev_c9[0] = c9
        return c9

    ln_act_prev = [0]
    ln_prev_c9 = [0]

    def transpose_tile(src_bf, trbuf, dst, dst_col, first_waits, evac0_waits=()):
        CHW = dst.shape[1] // NCH
        pe_c = []
        for c in range(NCH):
            pe_c.append(T(lambda t, c=c: t.transpose(trbuf[:, c * 128:(c + 1) * 128],
                                                     src_bf[:, c * 128:(c + 1) * 128],
                                                     identb[:, :]),
                          waits=first_waits if c == 0 else ()))
        if CHW == 128 and dst_col == 0:
            ev = A(lambda sc: sc.copy(dst[:, 0:NCH * 128], trbuf[:, 0:NCH * 128]),
                   waits=[(s_pe, pe_c[-1])] + list(evac0_waits))
            return pe_c[-1], [ev] * NCH
        evac_c = []
        for c in range(NCH):
            evac_c.append(A(lambda sc, c=c: sc.copy(
                dst[:, c * CHW + dst_col: c * CHW + dst_col + 128],
                trbuf[:, c * 128:(c + 1) * 128]),
                waits=([(s_pe, pe_c[-1])] + list(evac0_waits)) if c == 0 else []))
        return pe_c[-1], evac_c

    def project_tile(hfm_tile, wmat_b, ppb, evac_c, extra0):
        last = None
        for c in range(NCH):
            for n in range(2):
                w = []
                if n == 0 and evac_c is not None:
                    w.append((s_act, evac_c[c]))
                if n == 0 and c == 0:
                    w += list(extra0)
                last = T(lambda t, n=n, c=c: t.matmul(
                    pp[ppb][:, n * 512:(n + 1) * 512],
                    hfm_tile[:, c * 128:(c + 1) * 128],
                    wmat_b[:, c * D + n * 512: c * D + n * 512 + 512],
                    start=(c == 0), stop=(c == NCH - 1)),
                    waits=w)
        return last

    def rope_tile(ppb, tb, mm_cnt, tab_cnt):
        e1 = A(lambda sc: sc.copy(qef[:, :], pp[ppb][:, 0:1024:2]),
               waits=[(s_pe, mm_cnt), (s_dve, last_tab_dve[0])])
        e2 = A(lambda sc: sc.copy(qof[:, :], pp[ppb][:, 1:1024:2]))
        m1 = V(lambda v: v.tensor_tensor(pa_[:, :], qef[:, :], ctab[tb][:, :], op=ALU.mult),
               waits=[(s_act, e1), (s_ld, tab_cnt)])
        m2 = V(lambda v: v.tensor_tensor(pb_[:, :], qof[:, :], stab[tb][:, :], op=ALU.mult),
               waits=[(s_act, e2)])
        o1 = V(lambda v: v.tensor_tensor(ropeo[ppb][:, 0:1024:2], pa_[:, :], pb_[:, :],
                                         op=ALU.subtract),
               waits=[(s_dve, m2), (s_pe, last_ropeo_read[ppb])])
        m3 = V(lambda v: v.tensor_tensor(pa_[:, :], qef[:, :], stab[tb][:, :], op=ALU.mult),
               waits=[(s_dve, o1)])
        m4 = V(lambda v: v.tensor_tensor(pb_[:, :], qof[:, :], ctab[tb][:, :], op=ALU.mult))
        o2 = V(lambda v: v.tensor_tensor(ropeo[ppb][:, 1:1024:2], pa_[:, :], pb_[:, :],
                                         op=ALU.add), waits=[(s_dve, m4)])
        return o2, m2, e2

    last_apply = [0, 0]
    last_evac1 = [0, 0]
    last_evac2 = [0, 0]
    last_hfmb_use = [(s_pe, 0), (s_pe, 0)]
    last_hfmb_sp = [0, 0]
    last_pp_read = [0, 0]
    last_pp_act = [0, 0]
    last_tab_pool = [0, 0]
    last_tab_dve = [0, 0]
    last_lnb_read = [0, 0]
    last_ropeo_read = [0, 0]
    last_rope_pool = 0

    load_weight_bf16(wq_d, wbuf)
    c_wq = s_dve.count

    def p2_pass(ntiles, x_src, wcnt, cD, sD, dstFM, do_spill, spill_dict):
        nonlocal last_rope_pool
        c8s, c9s, ro_s = {}, {}, {}

        def ln_stat(t):
            xb = t % 2
            ld = DMA("sync", xbuf[xb][:, :], x_src[t * 128:(t + 1) * 128, :], s_ld,
                     waits=[(s_dve, last_apply[xb])])
            c8s[t] = layernorm_stat(xbuf[xb], [(s_ld, ld)])

        def ln_fin(t):
            xb = t % 2
            c9 = layernorm_fin(xbuf[xb], ropeb[xb][:, :], c8s[t],
                               apply_extra=[(s_pe, last_lnb_read[xb])])
            last_apply[xb] = c9
            c9s[t] = c9

        def mid(t):
            nonlocal last_rope_pool
            xb = t % 2
            pe_t, ev1 = transpose_tile(ropeb[xb], trb[xb], hfmb[xb], 0,
                                       [(s_dve, c9s[t]), (s_act, last_evac1[xb])],
                                       evac0_waits=[last_hfmb_use[xb],
                                                    (s_sp, last_hfmb_sp[xb])])
            last_lnb_read[xb] = pe_t
            last_evac1[xb] = ev1[-1]
            if do_spill:
                sp = DMA("gpsimd",
                         hfm_s.rearrange("(c p) T -> p c T", p=128)[:, :, t * 128:(t + 1) * 128],
                         hfmb[xb].rearrange("p (c f) -> p c f", c=NCH)[:, :, :], s_sp,
                         waits=[(s_act, ev1[-1])])
                spill_dict[t] = sp
                last_hfmb_sp[xb] = sp
            if t >= 2:
                back(t - 2)
            if t + 2 < ntiles:
                ln_fin(t + 2)
            if t + 3 < ntiles:
                ln_stat(t + 3)
            mm = project_tile(hfmb[xb], wbuf, xb, ev1,
                              [(s_dve, max(wcnt, last_pp_read[xb])),
                               (s_act, last_pp_act[xb])])
            last_hfmb_use[xb] = (s_pe, mm)
            DMA("sync", ctab[0][:, :], cD[t * 128:(t + 1) * 128, :], s_ld,
                waits=[(s_dve, max(last_tab_pool[0], last_tab_dve[0]))])
            tl2 = DMA("sync", stab[0][:, :], sD[t * 128:(t + 1) * 128, :], s_ld,
                      waits=[(s_dve, max(last_tab_pool[0], last_tab_dve[0]))])
            ro, rpool, e2c = rope_tile(xb, 0, mm, tl2)
            last_pp_act[xb] = e2c
            last_tab_pool[0] = rpool
            last_tab_dve[0] = ro
            last_rope_pool = rpool
            ro_s[t] = ro

        def back(t):
            xb = t % 2
            pe_q, ev2 = transpose_tile(ropeo[xb], trb2[xb], dstFM, t * 128,
                                       [(s_dve, ro_s[t]), (s_act, last_evac2[xb])])
            last_ropeo_read[xb] = pe_q
            last_evac2[xb] = ev2[-1]

        ln_stat(0)
        ln_fin(0)
        if ntiles > 1:
            ln_stat(1)
            ln_fin(1)
        if ntiles > 2:
            ln_stat(2)
        for t in range(ntiles):
            mid(t)
        if ntiles > 1:
            back(ntiles - 2)
        back(ntiles - 1)

    p2_pass(NTQ, xq_d, c_wq, cqq_d, sqq_d, Qfm, False, {})

    if upto == "p2q":
        p2trps.close(); p2psum.close(); p2buf.close(); kvq_scope.close()
        _finalize_stub()
        return _replay_and_return()

    c_lastqmm = s_pe.count
    load_weight_bf16(wk_d, wbuf, dest_free=[(s_pe, c_lastqmm)])
    c_wk = s_dve.count
    spill_cnt = {}
    p2_pass(NT, x_d, c_wk, cosk_d, sink_d, Kfm, True, spill_cnt)

    if upto == "p2k":
        p2trps.close(); p2psum.close(); p2buf.close(); kvq_scope.close()
        _finalize_stub()
        return _replay_and_return()

    c_lastkmm = s_pe.count
    load_weight_bf16(wv_d, wbuf, dest_free=[(s_pe, c_lastkmm)])
    c_wv = s_dve.count
    c_qk_act = s_act.count

    p2trps.close()
    nrm = ExitStack()
    psN = [nrm.enter_context(nc.psum_tensor(f"psN{i}", [128, 512], F32)) for i in range(2)]
    gmax = nrm.enter_context(nc.sbuf_tensor("gmax", [1, 8], F32))
    _sqv0 = nrm.enter_context(nc.sbuf_tensor("sqv0", [128, 512], BF16))
    sqv = [_sqv0, pa_]
    c_rp_final = s_dve.count
    cg0 = V(lambda v: v.memset(gmax[:, :], 0.0))

    lastN_dve = cg0
    rm_bank = [0, 0]
    sq_free = [s_pe.count, s_pe.count]

    def norm_slices():
        k = [0]
        pend = []

        comb_slot = [0, 0]

        def emit_rm(bk, gcol, mm):
            nonlocal lastN_dve
            sl2 = 6 + bk
            rm = V(lambda v: v.reduce_max(gmax[:, sl2:sl2 + 1], psN[bk][0:1, :], axis=AX.X),
                   waits=[(s_pe, mm), (s_dve, comb_slot[bk])])
            rm_bank[bk] = rm
            lastN_dve = V(lambda v: v.tensor_tensor(
                gmax[:, gcol:gcol + 1], gmax[:, gcol:gcol + 1], gmax[:, sl2:sl2 + 1],
                op=ALU.max), waits=[(s_dve, rm)])
            comb_slot[bk] = lastN_dve

        def make(src, gcol, last):
            def thunk():
                bk = k[0] % 2
                k[0] += 1
                if len(pend) >= 2:
                    emit_rm(*pend.pop(0))
                cs = V(lambda v: v.tensor_tensor(sqv[bk][:, :], src, src, op=ALU.mult),
                       waits=[(s_pe, sq_free[bk]), (s_act, c_qk_act),
                              (s_dve, c_rp_final)])
                mm = T(lambda t: t.matmul(psN[bk][0:1, :], onescol[:, :], sqv[bk][:, :],
                                          start=True, stop=True),
                       waits=[(s_dve, cs), (s_dve, rm_bank[bk])])
                sq_free[bk] = mm
                pend.append((bk, gcol, mm))
                if last:
                    while pend:
                        emit_rm(*pend.pop(0))
            return thunk

        out = []
        srcs = []
        for fm, width512, gcol in ((Qfm, QT // 512, 0), (Kfm, S // 512, 1)):
            for c in range(NCH):
                for sl in range(width512):
                    srcs.append((fm[:, (c * width512 + sl) * 512:
                                    (c * width512 + sl + 1) * 512], gcol))
        for i, (src, gcol) in enumerate(srcs):
            out.append(make(src, gcol, i == len(srcs) - 1))
        return out

    nslices = norm_slices()
    last_vproj = [0, 0]
    for t in range(NT):
        hb = t % 2
        ld = DMA("sync", hfmb[hb].rearrange("p (c f) -> p c f", c=NCH)[:, :, :],
                 hfm_s.rearrange("(c p) T -> p c T", p=128)[:, :, t * 128:(t + 1) * 128],
                 s_ld, waits=[(s_sp, spill_cnt[t]), (s_pe, last_vproj[hb]),
                              last_hfmb_use[hb], (s_sp, last_hfmb_sp[hb])])
        ppb = t % 2
        mm = project_tile(hfmb[hb], wbuf, ppb, None,
                          [(s_ld, ld), (s_dve, max(c_wv, last_pp_read[ppb])),
                           (s_act, last_pp_act[ppb])])
        last_vproj[hb] = mm
        last_hfmb_use[hb] = (s_pe, mm)
        for _ in range(3):
            if nslices:
                nslices.pop(0)()
        ca = A(lambda sc, t=t, ppb=ppb: sc.copy(Vtm[:, t * D: t * D + 512],
                                                pp[ppb][:, 0:512]),
               waits=[(s_pe, mm)])
        cv = A(lambda sc, t=t, ppb=ppb: sc.copy(
            Vtm[:, t * D + 512:(t + 1) * D], pp[ppb][:, 512:1024]))
        last_pp_act[ppb] = cv
    while nslices:
        nslices.pop(0)()

    c_p2_act = s_act.count
    c_p2_dve = s_dve.count

    if upto == "p2v":
        p2psum.close(); p2buf.close(); kvq_scope.close(); nrm.close()
        _finalize_stub()
        return _replay_and_return()


    cM1 = V(lambda v: v.tensor_tensor(gmax[:, 2:3], gmax[:, 0:1], gmax[:, 1:2], op=ALU.mult),
            waits=[(s_dve, lastN_dve)])
    aM1 = A(lambda sc: sc.activation(gmax[:, 3:4], gmax[:, 2:3], AF.Ln), waits=[(s_dve, cM1)])
    aM2 = A(lambda sc: sc.activation(gmax[:, 4:5], gmax[:, 3:4], AF.Exp, bias=0.0, scale=0.5),
            waits=[(s_act, aM1)])
    cM2 = V(lambda v: v.tensor_scalar(gmax[:, 5:6], gmax[:, 4:5], float(-SCALE), None,
                                      op0=ALU.mult), waits=[(s_act, aM2)])
    bM = T(lambda t: t.matmul(psN[0][:, 0:1], onesrow[:, :], gmax[:, 5:6], start=True, stop=True),
           waits=[(s_dve, cM2), (s_dve, rm_bank[0])])
    cM3 = V(lambda v: v.tensor_copy(mneg_bc[:, :], psN[0][:, 0:1]), waits=[(s_pe, bM)])

    lrow = nrm.enter_context(nc.sbuf_tensor("lrow", [1, 8], F32))
    cL1 = V(lambda v: v.tensor_tensor(qef[0:1, 0:32],
                                      lamv[0:1, 0:32], lamv[0:1, 32:64], op=ALU.mult),
            waits=[(s_dve, last_rope_pool)])
    cL2 = V(lambda v: v.tensor_tensor(qef[0:1, 32:64], lamv[0:1, 64:96], lamv[0:1, 96:128],
                                      op=ALU.mult))
    cL3 = V(lambda v: v.reduce_sum(lrow[0:1, 0:1], qef[0:1, 0:32], axis=AX.X),
            waits=[(s_dve, cL1)])
    cL4 = V(lambda v: v.reduce_sum(lrow[0:1, 1:2], qef[0:1, 32:64], axis=AX.X),
            waits=[(s_dve, cL2)])
    aL1 = A(lambda sc: sc.activation(lrow[0:1, 2:3], lrow[0:1, 0:1], AF.Exp),
            waits=[(s_dve, cL3)])
    aL2 = A(lambda sc: sc.activation(lrow[0:1, 3:4], lrow[0:1, 1:2], AF.Exp),
            waits=[(s_dve, cL4)])
    cL5 = V(lambda v: v.tensor_tensor(lrow[0:1, 4:5], lrow[0:1, 2:3], lrow[0:1, 3:4],
                                      op=ALU.subtract), waits=[(s_act, aL2)])
    cL6 = V(lambda v: v.tensor_scalar(lrow[0:1, 5:6], lrow[0:1, 4:5], float(LAMBDA_INIT), -1.0,
                                      op0=ALU.add, op1=ALU.mult), waits=[(s_dve, cL5)])
    bL = T(lambda t: t.matmul(psN[1][:, 0:1], onesrow[:, :], lrow[0:1, 5:6], start=True, stop=True),
           waits=[(s_dve, cL6), (s_dve, rm_bank[1])])
    cL7 = V(lambda v: v.tensor_copy(lamneg_bc[:, :], psN[1][:, 0:1]), waits=[(s_pe, bL)])
    cS1 = V(lambda v: v.memset(svec[:, :], 1.0))
    cS2 = V(lambda v: v.tensor_copy(svec[64:128, :], lamneg_bc[64:128, :]),
            waits=[(s_dve, cL7), (s_dve, cS1)])
    cS3 = V(lambda v: v.tensor_scalar(Lmat[:, :], istack[:, :], svec[:, 0:1], None, op0=ALU.mult),
            waits=[(s_dve, cS2), (s_ld, c_consts)])
    c_setup_done = max(cS3, cM3, c_p2_dve)

    nrm.close()
    p2psum.close()
    p2buf.close()

    if upto == "norms":
        kvq_scope.close()
        _finalize_stub()
        return _replay_and_return()

    att = ExitStack()
    psS = att.enter_context(nc.psum_tensor("psS", [128, 2048], F32))
    psA = [att.enter_context(nc.psum_tensor(f"psA{i}", [128, 512], F32)) for i in range(2)]
    psZ = [att.enter_context(nc.psum_tensor(f"psZ{i}", [128, 512], F32)) for i in range(2)]
    pt = att.enter_context(nc.sbuf_tensor("pt", [128, 2048], BF16))
    zrbuf = att.enter_context(nc.sbuf_tensor("zrbuf", [128, 512], F32))
    tbuf = att.enter_context(nc.sbuf_tensor("tbuf", [128, 512], F32))
    abuf = att.enter_context(nc.sbuf_tensor("abuf", [64, 512], F32))
    sqf = att.enter_context(nc.sbuf_tensor("sqf", [64, 512], F32))
    ubuf = att.enter_context(nc.sbuf_tensor("ubuf", [64, 512], F32))
    wlbuf = att.enter_context(nc.sbuf_tensor("wlbuf", [64, 512], F32))
    rstdb = att.enter_context(nc.sbuf_tensor("rstdb", [64, 512], F32))
    anbuf = att.enter_context(nc.sbuf_tensor("anbuf", [64, 512], BF16))

    NQC = QT // 512
    NGRP = 16 * NQC
    combine_free = {-2: 0, -1: 0}
    prev_spill = [0]

    def make_combine(m):
        bset = m % 2
        d, qc = m // NQC, m % NQC
        st = {}

        def s1():
            st["c1"] = V(lambda v: v.reciprocal(zrbuf[:, :], psZ[bset][:, :]),
                         waits=[(s_satt, (m + 1) * NT)])

        def s2():
            st["c2"] = V(lambda v: v.tensor_tensor(tbuf[:, :], psA[bset][:, :], zrbuf[:, :],
                                                   op=ALU.mult), waits=[(s_dve, st["c1"])])

        def s3():
            st["c4"] = T(lambda t: t.matmul(psZ[bset][0:64, :], Lmat[:, :], tbuf[:, :],
                                            start=True, stop=True), waits=[(s_dve, st["c2"])])

        def s4():
            st["c5a"] = V(lambda v: v.tensor_copy(abuf[:, :], psZ[bset][0:64, :]),
                          waits=[(s_pe, st["c4"])])

        def s5():
            st["c5b"] = V(lambda v: v.tensor_tensor(sqf[:, :], abuf[:, :], abuf[:, :],
                                                    op=ALU.mult), waits=[(s_dve, st["c5a"])])

        def s6():
            st["c6"] = T(lambda t: t.matmul(psZ[bset][0:64, :], ones64f[:, :], sqf[:, :],
                                            start=True, stop=True), waits=[(s_dve, st["c5b"])])

        def s7():
            st["c7"] = V(lambda v: v.tensor_scalar(ubuf[:, :], psZ[bset][0:64, :], 1.0 / 64,
                                                   float(EPS), op0=ALU.mult, op1=ALU.add),
                         waits=[(s_pe, st["c6"])])
            combine_free[m] = st["c7"]

        def s8():
            st["c8"] = A(lambda sc: sc.activation(wlbuf[:, :], ubuf[:, :], AF.Ln),
                         waits=[(s_dve, st["c7"])])

        def s9():
            st["c9"] = A(lambda sc: sc.activation(rstdb[:, :], wlbuf[:, :], AF.Exp,
                                                  bias=0.0, scale=-0.5),
                         waits=[(s_act, st["c8"])])

        def s10():
            st["c10"] = V(lambda v: v.tensor_tensor(anbuf[:, :], abuf[:, :], rstdb[:, :],
                                                    op=ALU.mult),
                          waits=[(s_act, st["c9"]), (s_sp, prev_spill[0])])

        def s11():
            prev_spill[0] = DMA("gpsimd", attn_s[d * 64:(d + 1) * 64,
                                                 qc * 512:(qc + 1) * 512], anbuf[:, :],
                                s_sp, waits=[(s_dve, st["c10"])])

        return [s1, s2, s3, s4, s5, s6, s7, s8, s9, s10, s11]

    pending = []
    for n in range(NGRP):
        d, qc = n // NQC, n % NQC
        g = d // 2
        r0 = 32 * ((2 * d) % 4)
        r1 = r0 + 32
        aset = n % 2
        grp_exp_base = s_se.count
        if n >= 1:
            pending = make_combine(n - 1)
        for j in range(NT):
            p = j % 2
            waits = [(s_se, grp_exp_base + j - 1)]
            if j == 0:
                waits.append((s_dve, max(combine_free[n - 2], c_setup_done)))
                waits.append((s_act, c_p2_act))
            T(lambda t, p=p, r0=r0, g=g, j=j, qc=qc: t.matmul(
                psS[:, p * 1024: p * 1024 + 512],
                Kfm[r0:r0 + 32, g * S + j * 128: g * S + (j + 1) * 128],
                Qfm[r0:r0 + 32, g * QT + qc * 512: g * QT + (qc + 1) * 512],
                start=True, stop=True, tile_position=(r0, 0)), waits=waits)
            smm = T(lambda t, p=p, r1=r1, g=g, j=j, qc=qc: t.matmul(
                psS[:, p * 1024 + 512: p * 1024 + 1024],
                Kfm[r1:r1 + 32, g * S + j * 128: g * S + (j + 1) * 128],
                Qfm[r1:r1 + 32, g * QT + qc * 512: g * QT + (qc + 1) * 512],
                start=True, stop=True, tile_position=(r1, 0)), inc=s_smm)
            if j >= 1:
                jj = j - 1
                pm = jj % 2
                T(lambda t, jj=jj, pm=pm, d=d, aset=aset: t.matmul(
                    psA[aset][0:64, :], Vtm[:, jj * D + d * 64: jj * D + d * 64 + 64],
                    pt[:, pm * 1024: pm * 1024 + 512],
                    start=(jj == 0), stop=False, tile_position=(0, 0)),
                    waits=[(s_se, grp_exp_base + jj + 1)])
                T(lambda t, jj=jj, pm=pm, d=d, aset=aset: t.matmul(
                    psA[aset][64:128, :], Vtm[:, jj * D + d * 64: jj * D + d * 64 + 64],
                    pt[:, pm * 1024 + 512: pm * 1024 + 1024],
                    start=(jj == 0), stop=False, tile_position=(0, 64),
                    skip_group_check=True))
                T(lambda t, jj=jj, pm=pm, aset=aset: t.matmul(
                    psZ[aset][0:64, :], ones64b[:, :],
                    pt[:, pm * 1024: pm * 1024 + 512],
                    start=(jj == 0), stop=False, tile_position=(0, 0)))
                prog.op("tensor", lambda t, jj=jj, pm=pm, aset=aset: t.matmul(
                    psZ[aset][64:128, :], ones64b[:, :],
                    pt[:, pm * 1024 + 512: pm * 1024 + 1024],
                    start=(jj == 0), stop=False, tile_position=(0, 64),
                    skip_group_check=True), inc=s_satt)
            wexp = [(s_smm, smm)]
            if j >= 2:
                wexp.append((s_satt, s_satt.count - 1))
            A(lambda sc, p=p: sc.activation(pt[:, p * 1024: p * 1024 + 1024],
                                            psS[:, p * 1024: p * 1024 + 1024],
                                            AF.Exp, bias=mneg_bc[:, 0:1], scale=float(SCALE)),
              waits=wexp, inc=s_se)
            if pending and j >= 4 and (j % 2 == 0):
                pending.pop(0)()
        jj = NT - 1
        pm = jj % 2
        T(lambda t, jj=jj, pm=pm, d=d, aset=aset: t.matmul(
            psA[aset][0:64, :], Vtm[:, jj * D + d * 64: jj * D + d * 64 + 64],
            pt[:, pm * 1024: pm * 1024 + 512],
            start=False, stop=True, tile_position=(0, 0)),
            waits=[(s_se, grp_exp_base + jj + 1)])
        T(lambda t, jj=jj, pm=pm, d=d, aset=aset: t.matmul(
            psA[aset][64:128, :], Vtm[:, jj * D + d * 64: jj * D + d * 64 + 64],
            pt[:, pm * 1024 + 512: pm * 1024 + 1024],
            start=False, stop=True, tile_position=(0, 64), skip_group_check=True))
        T(lambda t, jj=jj, pm=pm, aset=aset: t.matmul(
            psZ[aset][0:64, :], ones64b[:, :], pt[:, pm * 1024: pm * 1024 + 512],
            start=False, stop=True, tile_position=(0, 0)))
        prog.op("tensor", lambda t, jj=jj, pm=pm, aset=aset: t.matmul(
            psZ[aset][64:128, :], ones64b[:, :], pt[:, pm * 1024 + 512: pm * 1024 + 1024],
            start=False, stop=True, tile_position=(0, 64), skip_group_check=True),
            inc=s_satt)
        while pending:
            pending.pop(0)()

    for thunk in make_combine(NGRP - 1):
        thunk()
    c_all_spills = s_sp.count

    att.close()
    kvq_scope.close()

    if upto == "att":
        _finalize_stub()
        return _replay_and_return()

    c_att_pe = s_pe.count
    c_att_act = s_act.count
    c_att_dve = s_dve.count
    p4 = ExitStack()
    wo_b = p4.enter_context(nc.sbuf_tensor("wo_b", [128, NCH * D], BF16))
    b0row = p4.enter_context(nc.sbuf_tensor("b0row", [1, D], F32))
    b1row = p4.enter_context(nc.sbuf_tensor("b1row", [1, D], F32))
    gwf = p4.enter_context(nc.sbuf_tensor("gwf", [128, 64], F32))
    identf = p4.enter_context(nc.sbuf_tensor("identf_s", [128, 128], F32))
    wst4 = [p4.enter_context(nc.sbuf_tensor(f"wst4_{i}", [128, 512], F32)) for i in range(2)]
    DMA("sync", b0row[:, :], b0_d[:, :], s_ld,
        waits=[(s_pe, c_att_pe), (s_act, c_att_act), (s_dve, c_att_dve)])
    DMA("sync", b1row[:, :], b1_d[:, :], s_ld)
    DMA("sync", identf[:, :], identf_d[:, :], s_ld)
    for cc in range(8):
        DMA("sync", gwf[:, cc * 8:(cc + 1) * 8], gw_d[cc * 128:(cc + 1) * 128, :], s_ld)
    c_gw = s_ld.count
    c_wob = 0
    _wo_stage_last = [(s_dve, 0), (s_dve, 0)]
    for ch in range(2 * NCH):
        cc, hh = ch // 2, ch % 2
        bb = ch % 2
        ldw = DMA("sync", wst4[bb][:, :],
                  wo_d[cc * 128:(cc + 1) * 128, hh * 512:(hh + 1) * 512], s_ld,
                  waits=[_wo_stage_last[bb]])
        dcol = cc * D + hh * 512
        c_wob = V(lambda v, dcol=dcol, bb=bb: v.tensor_scalar(
            wo_b[:, dcol:dcol + 512], wst4[bb][:, :], gg[:, 0:1], None, op0=ALU.mult),
            waits=[(s_ld, ldw), (s_dve, c_gg)])
        _wo_stage_last[bb] = (s_dve, c_wob)
    x1_all = p4.enter_context(nc.sbuf_tensor("x1_all", [128, NTQ * D], F32))
    h2_all = p4.enter_context(nc.sbuf_tensor("h2_all", [128, NTQ * D], BF16))
    h2f = [p4.enter_context(nc.sbuf_tensor(f"h2f{i}", [128, D], F32)) for i in range(2)]
    s01 = p4.enter_context(nc.sbuf_tensor("s01", [128, 2 * NTQ], F32))
    atile = [p4.enter_context(nc.sbuf_tensor(f"atile{i}", [128, D], BF16)) for i in range(2)]
    xqb = [p4.enter_context(nc.sbuf_tensor(f"xqb{i}", [128, D], F32)) for i in range(2)]
    h2tm = [p4.enter_context(nc.sbuf_tensor(f"h2tm{i}", [128, D], F32)) for i in range(2)]
    gstat = p4.enter_context(nc.sbuf_tensor("gstat", [128, 16], F32))
    gbuf = p4.enter_context(nc.sbuf_tensor("gbuf", [128, 8], F32))
    gbuf2 = p4.enter_context(nc.sbuf_tensor("gbuf2", [128, 8], F32))
    stat4 = p4.enter_context(nc.sbuf_tensor("stat4", [128, 8], F32))
    pax = p4.enter_context(nc.sbuf_tensor("pax", [128, 512], BF16))
    w0T = p4.enter_context(nc.sbuf_tensor("w0T", [128, NCH * D], BF16))
    w1T = p4.enter_context(nc.sbuf_tensor("w1T", [128, NCH * D], BF16))
    wrow = p4.enter_context(nc.sbuf_tensor("wrow", [128, D], BF16))

    p4a_ps = ExitStack()
    psXO = p4a_ps.enter_context(nc.psum_tensor("psXO", [128, 1024], F32))
    psTR = [p4a_ps.enter_context(nc.psum_tensor(f"psTR{i}", [128, 1024], F32))
            for i in range(2)]
    psG = p4a_ps.enter_context(nc.psum_tensor("psG", [128, 8], F32))
    trb4 = p4a_ps.enter_context(nc.psum_tensor("trb4", [128, 1024], BF16))

    ln4_act_prev = [0]
    ln4_prev_c9 = [0]

    def ln4_stat(x_ap, first_waits):
        c1 = V(lambda v: v.reduce_sum(stat4[:, 0:1], x_ap, axis=AX.X), waits=first_waits)
        t1 = V(lambda v: v.scalar_tensor_tensor(pax[:, :], x_ap[:, 0:512], 1.0, x_ap[:, 0:512],
                                                op0=ALU.mult, op1=ALU.mult,
                                                accum_out=stat4[:, 1:2]))
        t2 = V(lambda v: v.scalar_tensor_tensor(pax[:, :], x_ap[:, 512:1024], 1.0,
                                                x_ap[:, 512:1024], op0=ALU.mult, op1=ALU.mult,
                                                accum_out=stat4[:, 7:8]), waits=[(s_dve, t1)])
        c4 = V(lambda v: v.tensor_tensor(stat4[:, 1:2], stat4[:, 1:2], stat4[:, 7:8],
                                         op=ALU.add), waits=[(s_dve, t2)])
        c5 = V(lambda v: v.tensor_scalar(stat4[:, 2:3], stat4[:, 0:1], 1.0 / D, None,
                                         op0=ALU.mult), waits=[(s_dve, c1)])
        c6 = V(lambda v: v.tensor_tensor(stat4[:, 3:4], stat4[:, 2:3], stat4[:, 2:3],
                                         op=ALU.mult), waits=[(s_dve, c5)])
        c7 = V(lambda v: v.tensor_scalar(stat4[:, 4:5], stat4[:, 1:2], 1.0 / D, float(EPS),
                                         op0=ALU.mult, op1=ALU.add),
               waits=[(s_dve, c4), (s_act, ln4_act_prev[0])])
        c8 = V(lambda v: v.tensor_tensor(stat4[:, 4:5], stat4[:, 4:5], stat4[:, 3:4],
                                         op=ALU.subtract), waits=[(s_dve, c7), (s_dve, c6)])
        return c8

    def ln4_fin(x_ap, out_ap, c8cnt, apply_extra=()):
        a1 = A(lambda sc: sc.activation(stat4[:, 5:6], stat4[:, 4:5], AF.Ln),
               waits=[(s_dve, c8cnt), (s_dve, ln4_prev_c9[0])])
        a2 = A(lambda sc: sc.activation(stat4[:, 6:7], stat4[:, 5:6], AF.Exp, bias=0.0,
                                        scale=-0.5), waits=[(s_act, a1)])
        ln4_act_prev[0] = a1
        c9 = V(lambda v: v.tensor_scalar(out_ap, x_ap, stat4[:, 2:3], stat4[:, 6:7],
                                         op0=ALU.subtract, op1=ALU.mult),
               waits=[(s_act, a2)] + list(apply_extra))
        ln4_prev_c9[0] = c9
        return c9

    last_at = [0, 0]
    last_xq = [0, 0]
    last_h2tm = [0, 0]
    last_psTR = [0, 0]
    last_h2f = [0, 0]
    prev_g12 = 0
    prev_g8 = 0
    prev_g_act = 0
    prev_g_dve = 0
    prev_cx1 = 0
    p4_dmas, p4_mm, p4_c9, p4_pe, p4_ae, p4_ve, p4_gm = {}, {}, {}, {}, {}, {}, {}

    def p4_dma(t):
        ab = t % 2
        lda = DMA("sync", atile[ab].rearrange("p (c f) -> p c f", c=NCH)[:, :, :],
                  attn_s.rearrange("(c p) T -> p c T", p=128)[:, :, t * 128:(t + 1) * 128],
                  s_ld, waits=[(s_pe, last_at[ab]), (s_sp, c_all_spills)])
        ldx = DMA("sync", xqb[ab][:, :], xq_d[t * 128:(t + 1) * 128, :], s_ld,
                  waits=[(s_dve, last_xq[ab])])
        p4_dmas[t] = (lda, ldx)

    def p4_mmwo(t):
        ab = t % 2
        mm = None
        for n in range(2):
            for c in range(NCH):
                w = [(s_ld, p4_dmas[t][0]),
                     (s_dve, max(c_wob, prev_cx1))] if (n == 0 and c == 0) else ()
                mm = T(lambda t_, n=n, c=c, ab=ab: t_.matmul(
                    psXO[:, n * 512:(n + 1) * 512], atile[ab][:, c * 128:(c + 1) * 128],
                    wo_b[:, c * D + n * 512: c * D + n * 512 + 512],
                    start=(c == 0), stop=(c == NCH - 1)), waits=w)
        last_at[ab] = mm
        p4_mm[t] = mm

    def p4_ln(t):
        nonlocal prev_cx1
        ab = t % 2
        cx1 = V(lambda v, t=t, ab=ab: v.tensor_tensor(x1_all[:, t * D:(t + 1) * D],
                                                      psXO[:, :], xqb[ab][:, :],
                                                      op=ALU.add),
                waits=[(s_pe, p4_mm[t]), (s_ld, p4_dmas[t][1])])
        prev_cx1 = cx1
        last_xq[ab] = cx1
        c8 = ln4_stat(x1_all[:, t * D:(t + 1) * D], [(s_dve, cx1)])
        c9 = ln4_fin(x1_all[:, t * D:(t + 1) * D], h2tm[ab][:, :], c8,
                     apply_extra=[(s_pe, last_h2tm[ab])])
        p4_c9[t] = c9

    def p4_tr(t):
        ab = t % 2
        pe_last = None
        for c in range(NCH):
            w = [(s_dve, p4_c9[t]), (s_dve, last_psTR[ab])] if c == 0 else ()
            pe_last = T(lambda t_, c=c, ab=ab: t_.transpose(
                psTR[ab][:, c * 128:(c + 1) * 128],
                h2tm[ab][:, c * 128:(c + 1) * 128],
                identf[:, :]), waits=w)
        last_h2tm[ab] = pe_last
        aev = A(lambda sc, t=t, ab=ab: sc.copy(
            h2_all[:, t * D:(t + 1) * D], psTR[ab][:, :]),
            waits=[(s_pe, pe_last)])
        p4_pe[t] = pe_last
        p4_ae[t] = aev

    def p4_vev(t):
        ab = t % 2
        vev = V(lambda v, ab=ab: v.tensor_copy(h2f[ab][:, :], psTR[ab][:, :]),
                waits=[(s_pe, p4_pe[t]), (s_act, p4_ae[t]),
                       (s_pe, last_h2f[ab])])
        last_psTR[ab] = vev
        p4_ve[t] = vev

    def p4_gmm(t):
        ab = t % 2
        gmm = None
        for c in range(NCH):
            w = [(s_dve, p4_ve[t]), (s_ld, c_gw), (s_dve, prev_g8),
                 (s_act, prev_g_act)] if c == 0 else ()
            gmm = T(lambda t_, c=c, ab=ab: t_.matmul(
                psG[:, :], h2f[ab][:, c * 128:(c + 1) * 128],
                gwf[:, c * 8:(c + 1) * 8],
                start=(c == 0), stop=(c == NCH - 1)), waits=w)
        last_h2f[ab] = gmm
        p4_gm[t] = gmm

    def p4_gchain(t):
        nonlocal prev_g12, prev_g8, prev_g_act, prev_g_dve
        gmm = p4_gm[t]
        g1 = V(lambda v: v.reduce_max(gstat[:, 0:1], psG[:, :], axis=AX.X),
               waits=[(s_pe, gmm), (s_act, prev_g12)])
        g2 = V(lambda v: v.tensor_scalar(gstat[:, 1:2], gstat[:, 0:1], -1.0, None,
                                         op0=ALU.mult),
               waits=[(s_dve, g1), (s_act, prev_g_act)])
        g3 = A(lambda sc: sc.activation(gbuf[:, :], psG[:, :], AF.Exp, bias=gstat[:, 1:2],
                                        scale=1.0), waits=[(s_dve, g2)])
        g4 = V(lambda v: v.reduce_sum(gstat[:, 2:3], gbuf[:, :], axis=AX.X),
               waits=[(s_act, g3)])
        g5a = A(lambda sc: sc.activation(gstat[:, 10:11], gstat[:, 2:3], AF.Ln),
                waits=[(s_dve, g4)])
        g5 = A(lambda sc: sc.activation(gstat[:, 3:4], gstat[:, 10:11], AF.Exp,
                                        bias=0.0, scale=-1.0), waits=[(s_act, g5a)])
        g6 = V(lambda v: v.tensor_scalar(gstat[:, 4:5], psG[:, 0:1], gstat[:, 0:1], None,
                                         op0=ALU.is_equal), waits=[(s_dve, g1), (s_act, g3)])
        g7 = V(lambda v: v.tensor_scalar(gbuf2[:, :], psG[:, :], gstat[:, 0:1], None,
                                         op0=ALU.is_equal), waits=[(s_act, g3)])
        g8 = V(lambda v: v.scalar_tensor_tensor(gbuf2[:, :], gbuf2[:, :], -1e30, psG[:, :],
                                                op0=ALU.mult, op1=ALU.add),
               waits=[(s_dve, g7)])
        g9 = V(lambda v: v.reduce_max(gstat[:, 5:6], gbuf2[:, :], axis=AX.X),
               waits=[(s_dve, g8)])
        g10 = V(lambda v: v.tensor_tensor(gstat[:, 6:7], gbuf2[:, 1:2], gstat[:, 5:6],
                                          op=ALU.is_equal), waits=[(s_dve, g9)])
        g11 = V(lambda v: v.tensor_tensor(gstat[:, 7:8], gstat[:, 5:6], gstat[:, 0:1],
                                          op=ALU.subtract), waits=[(s_dve, g9)])
        g12 = A(lambda sc: sc.activation(gstat[:, 8:9], gstat[:, 7:8], AF.Exp),
                waits=[(s_dve, g11)])
        prev_g12 = g12
        g13 = V(lambda v, t=t: v.tensor_tensor(s01[:, 2 * t:2 * t + 1], gstat[:, 4:5],
                                               gstat[:, 3:4], op=ALU.mult),
                waits=[(s_dve, g6), (s_act, g5)])
        g14 = V(lambda v: v.tensor_tensor(gstat[:, 9:10], gstat[:, 6:7], gstat[:, 8:9],
                                          op=ALU.mult), waits=[(s_act, g12), (s_dve, g10)])
        g15 = V(lambda v, t=t: v.tensor_tensor(s01[:, 2 * t + 1:2 * t + 2], gstat[:, 9:10],
                                               gstat[:, 3:4], op=ALU.mult),
                waits=[(s_dve, g14)])
        prev_g8 = g8
        prev_g_act = g3
        prev_g_dve = g15

    wT_state = {"cast": 0, "evl": 0, "trs": 0, "ldp": c_wob}

    def wT_row(idx):
        wd, dest = (w0_d, w0T) if idx < NCH else (w1_d, w1T)
        r = idx % NCH
        lds = 0
        for hh in range(2):
            lds = DMA("sync", wst4[hh][:, :],
                      wd[r * 128:(r + 1) * 128, hh * 512:(hh + 1) * 512], s_ld,
                      waits=[(s_dve, max(wT_state["cast"], wT_state["ldp"]))])
        cast0 = V(lambda v: v.tensor_copy(wrow[:, 0:512], wst4[0][:, :]),
                  waits=[(s_ld, lds), (s_pe, wT_state["trs"])])
        cast = V(lambda v: v.tensor_copy(wrow[:, 512:1024], wst4[1][:, :]),
                 waits=[(s_dve, cast0)])
        pl = None
        for c in range(NCH):
            w = [(s_dve, cast), (s_act, wT_state["evl"])] if c == 0 else ()
            pl = T(lambda t_, c=c: t_.transpose(trb4[:, c * 128:(c + 1) * 128],
                                                wrow[:, c * 128:(c + 1) * 128],
                                                identb[:, :]), waits=w)
        evl = None
        for c in range(NCH):
            evl = A(lambda sc, c=c, r=r, dest=dest: sc.copy(
                dest[:, c * D + r * 128: c * D + (r + 1) * 128],
                trb4[:, c * 128:(c + 1) * 128]),
                waits=[(s_pe, pl)] if c == 0 else ())
        wT_state["cast"] = cast
        wT_state["evl"] = evl
        wT_state["trs"] = pl
        return evl

    p4_dma(0)
    if NTQ > 1:
        p4_dma(1)
    p4_mmwo(0)
    c_w1T = 0
    for t in range(NTQ):
        if t >= 3:
            p4_gchain(t - 3)
        if t < 2 * NCH:
            c_w1T = wT_row(t)
        p4_ln(t)
        if t + 1 < NTQ:
            p4_mmwo(t + 1)
        if t + 2 < NTQ:
            p4_dma(t + 2)
        if t >= 1:
            p4_tr(t - 1)
        if t >= 2:
            p4_gmm(t - 2)
        if t >= 1:
            p4_vev(t - 1)
    p4_tr(NTQ - 1)
    p4_vev(NTQ - 1)
    if NTQ >= 3:
        p4_gchain(NTQ - 3)
    p4_gmm(NTQ - 2)
    if NTQ >= 2:
        p4_gchain(NTQ - 2)
    p4_gmm(NTQ - 1)
    p4_gchain(NTQ - 1)
    for idx in range(min(NTQ, 2 * NCH), 2 * NCH):
        c_w1T = wT_row(idx)

    c_4a_act = s_act.count
    c_4a_dve = s_dve.count
    p4a_ps.close()

    p4b_ps = ExitStack()
    psE = [p4b_ps.enter_context(nc.psum_tensor(f"psE{i}", [128, 1024], F32)) for i in range(2)]
    ebuf = [p4.enter_context(nc.sbuf_tensor(f"ebuf{i}", [128, D], F32)) for i in range(2)]
    ybuf = [p4.enter_context(nc.sbuf_tensor(f"ybuf{i}", [128, D], F32)) for i in range(2)]
    last_sil = [0, 0]
    last_store = [0, 0]
    last_fb = 0
    for t in range(NTQ):
        for e in range(2):
            wT = w0T if e == 0 else w1T
            brow = b0row if e == 0 else b1row
            mm = None
            for n in range(2):
                for c in range(NCH):
                    w = ()
                    if n == 0 and c == 0:
                        w = [(s_act, max(last_sil[e], c_w1T, c_4a_act)), (s_ld, c_consts)]
                    mm = T(lambda t_, n=n, c=c, t=t, wT=wT, e=e: t_.matmul(
                        psE[e][:, n * 512:(n + 1) * 512],
                        h2_all[:, t * D + c * 128: t * D + (c + 1) * 128],
                        wT[:, c * D + n * 512: c * D + n * 512 + 512],
                        start=(c == 0), stop=(c == NCH - 1)), waits=w)
                mm = T(lambda t_, n=n, e=e, brow=brow: t_.matmul(
                    psE[e][:, n * 512:(n + 1) * 512], onesrow[:, :],
                    brow[0:1, n * 512:(n + 1) * 512], start=False, stop=False,
                    skip_group_check=True))
            last_sil[e] = A(lambda sc, e=e: sc.activation(ebuf[e][:, :], psE[e][:, :], AF.Silu),
                            waits=[(s_pe, mm), (s_dve, last_fb)])
        yb = t % 2
        f1 = V(lambda v, t=t: v.tensor_scalar(ebuf[0][:, :], ebuf[0][:, :],
                                              s01[:, 2 * t:2 * t + 1], None, op0=ALU.mult),
               waits=[(s_act, last_sil[1]), (s_sp, max(last_store))])
        f2 = V(lambda v, t=t, yb=yb: v.scalar_tensor_tensor(
            ybuf[yb][:, :], ebuf[1][:, :], s01[:, 2 * t + 1:2 * t + 2], ebuf[0][:, :],
            op0=ALU.mult, op1=ALU.add), waits=[(s_dve, f1)])
        f3 = V(lambda v, t=t, yb=yb: v.tensor_tensor(ybuf[yb][:, :], ybuf[yb][:, :],
                                                     x1_all[:, t * D:(t + 1) * D], op=ALU.add),
               waits=[(s_dve, f2)])
        last_fb = f3
        st = DMA("gpsimd", out_d[t * 128:(t + 1) * 128, :], ybuf[yb][:, :], s_sp,
                 waits=[(s_dve, f3)])
        last_store[yb] = st

    final_store = s_sp.count
    prog.op("gpsimd", lambda gps: gps.nop(), waits=[(s_sp, final_store)])

    p4b_ps.close()
    p4.close()

    return _replay_and_return()


def _rope_tables():
    d = 32
    e = (2.0 * np.arange(d // 2, dtype=np.float32) / np.float32(d)).astype(np.float32)
    freq = (np.float32(1.0) / (np.float32(0.0001) ** e)).astype(np.float32)
    pos = np.arange(S, dtype=np.float32)[:, None]
    ang = (pos * freq[None, :]).astype(np.float32)
    cos = np.cos(ang).astype(np.float32)
    sin = np.sin(ang).astype(np.float32)
    return np.tile(cos, (1, 32)), np.tile(sin, (1, 32))


_NC_CACHE = {}


def kernel(L0_attn_x, L0_attn_Wq, L0_attn_Wk, L0_attn_Wv, L0_attn_Wo,
           L0_attn_lambda_q1, L0_attn_lambda_k1, L0_attn_lambda_q2, L0_attn_lambda_k2,
           L0_attn_subln_g, L0_ffn_gate_w, L0_ffn_w0, L0_ffn_b0, L0_ffn_w1, L0_ffn_b1):
    x = np.ascontiguousarray(np.asarray(L0_attn_x, dtype=np.float32))
    Wq = np.ascontiguousarray(np.asarray(L0_attn_Wq, dtype=np.float32))
    Wk = np.ascontiguousarray(np.asarray(L0_attn_Wk, dtype=np.float32))
    Wv = np.ascontiguousarray(np.asarray(L0_attn_Wv, dtype=np.float32))
    Wo = np.ascontiguousarray(np.asarray(L0_attn_Wo, dtype=np.float32))
    w0 = np.ascontiguousarray(np.asarray(L0_ffn_w0, dtype=np.float32))
    w1 = np.ascontiguousarray(np.asarray(L0_ffn_w1, dtype=np.float32))
    gw = np.ascontiguousarray(np.asarray(L0_ffn_gate_w, dtype=np.float32))
    b0 = np.asarray(L0_ffn_b0, dtype=np.float32).reshape(1, D)
    b1 = np.asarray(L0_ffn_b1, dtype=np.float32).reshape(1, D)
    g = np.asarray(L0_attn_subln_g, dtype=np.float32)
    lamv = np.concatenate([np.asarray(L0_attn_lambda_q1, dtype=np.float32),
                           np.asarray(L0_attn_lambda_k1, dtype=np.float32),
                           np.asarray(L0_attn_lambda_q2, dtype=np.float32),
                           np.asarray(L0_attn_lambda_k2, dtype=np.float32)]).reshape(1, 128)
    cosT, sinT = _rope_tables()
    coskb = cosT.astype(ml_dtypes.bfloat16)
    sinkb = sinT.astype(ml_dtypes.bfloat16)

    identb = np.eye(128, dtype=np.float32).astype(ml_dtypes.bfloat16)
    identf = np.eye(128, dtype=np.float32)
    ones64b = np.ones((128, 64), dtype=np.float32).astype(ml_dtypes.bfloat16)
    ones64f = np.ones((64, 64), dtype=np.float32)
    onesrow = np.ones((1, 128), dtype=np.float32)
    onescol = np.ones((128, 1), dtype=np.float32).astype(ml_dtypes.bfloat16)
    istack = np.concatenate([np.eye(64, dtype=np.float32),
                             np.eye(64, dtype=np.float32)], axis=0)
    gvec = np.tile(g, 2).reshape(128, 1).astype(np.float32)

    if "nc" not in _NC_CACHE:
        _NC_CACHE["nc"] = build_program()
    nc = _NC_CACHE["nc"]

    in_maps = []
    for core in range(8):
        b = core // 2
        qh = core % 2
        q0 = qh * QT
        in_maps.append({
            "x": x[b],
            "xq": np.ascontiguousarray(x[b, q0:q0 + QT, :]),
            "wq": Wq, "wk": Wk, "wv": Wv, "wo": Wo,
            "w0": w0, "w1": w1, "gw": gw, "b0": b0, "b1": b1,
            "gvec": gvec, "lamv": lamv,
            "cosk": coskb, "sink": sinkb,
            "cosqq": np.ascontiguousarray(coskb[q0:q0 + QT, :]),
            "sinqq": np.ascontiguousarray(sinkb[q0:q0 + QT, :]),
            "identb": identb, "identf": identf,
            "ones64b": ones64b, "ones64f": ones64f,
            "onesrow": onesrow, "onescol": onescol,
            "istack": istack,
        })

    res = run_bass_kernel_spmd(nc, in_maps, core_ids=list(range(8)))

    out = np.empty((B, S, D), dtype=np.float32)
    for core in range(8):
        b = core // 2
        qh = core % 2
        out[b, qh * QT:(qh + 1) * QT, :] = res.results[core]["out"]
    return out
